# version 12
# baseline (speedup 1.0000x reference)
import contextlib
import numpy as np
import concourse.bass as bass
import concourse.mybir as mybir
from concourse.bass_utils import run_bass_kernel_spmd

F32 = mybir.dt.float32
BF16 = mybir.dt.bfloat16
AF = mybir.ActivationFunctionType
ALU = mybir.AluOpType
AX = mybir.AxisListType

NCORES = 8
D = 1024
DC = 8
T = 2048
NTT = 16
NST = 64
CH = 64
TB = 1024
NTB = T // TB
HID = 2048
KCN = 16
E = 8
FT = 512
NFT = T // FT
DTR = 64
BIGPOS = 1.0e30
EPS_LN = 1e-5
EPS_RMS = 1e-6
Z_ON_GP_MOD = 1000000


class Prog:
    ENGINES = ("sync", "scalar", "vector", "tensor", "gpsimd")
    SEM_OF = {"sync": "SP", "scalar": "ACT", "vector": "DVE",
              "tensor": "PE", "gpsimd": "GP"}
    DMA_SEM_OF = {"sync": "SP", "scalar": "AD", "gpsimd": "GPD"}
    DMA_SEMS = ("SP", "AD", "GPD")

    def __init__(self):
        self.items = {e: [] for e in self.ENGINES}
        self.cnt = {s: 0 for s in ("SP", "ACT", "DVE", "PE", "GP", "AD", "GPD")}

    def op(self, eng, fn, waits=(), inc=1):
        sem = self.SEM_OF[eng]
        if inc:
            self.cnt[sem] += 1
        self.items[eng].append((fn, [w for w in waits if w and w[1] > 0],
                                sem if inc else None))
        return self.cnt[sem]

    def dma(self, eng, fn, waits=()):
        sem = self.DMA_SEM_OF[eng]
        self.cnt[sem] += 16
        self.items[eng].append((fn, [w for w in waits if w and w[1] > 0], sem))
        return self.cnt[sem]

    def state(self):
        return dict(self.cnt)

    def emit(self, nc):
        sems = {}
        ctx = contextlib.ExitStack()
        for name in self.cnt:
            sems[name] = ctx.enter_context(nc.semaphore("sem_" + name))
        final_waits = [(s, v) for s, v in self.cnt.items() if v > 0]

        def run(eng_name, e):
            own = self.SEM_OF[eng_name]
            last = {}
            for fn, waits, sem in self.items[eng_name]:
                for (ws, wv) in waits:
                    if ws == own and ws not in self.DMA_SEMS:
                        continue
                    if last.get(ws, -1) >= wv:
                        continue
                    e.wait_ge(sems[ws], wv)
                    last[ws] = wv
                inst = fn(e)
                if sem is not None:
                    inst.then_inc(sems[sem], 16 if sem in self.DMA_SEMS else 1)
            for ws, wv in final_waits:
                if ws == own and ws not in self.DMA_SEMS:
                    continue
                if last.get(ws, -1) >= wv:
                    continue
                e.wait_ge(sems[ws], wv)
                last[ws] = wv

        with nc.Block() as block:
            @block.sync
            def _(e):
                run("sync", e)

            @block.scalar
            def _(e):
                run("scalar", e)

            @block.vector
            def _(e):
                run("vector", e)

            @block.tensor
            def _(e):
                run("tensor", e)

            @block.gpsimd
            def _(e):
                run("gpsimd", e)
        return ctx


class Guard:

    def __init__(self, nslots):
        self.n = nslots
        self.v = [[] for _ in range(nslots)]

    def waits(self, i):
        return list(self.v[i % self.n])

    def set(self, i, *deps):
        self.v[i % self.n] = [d for d in deps if d and d[1] > 0]


DEBUG = False


def build_nc(debug=False):
    nc = bass.Bass("TRN2", target_bir_lowering=False,
                   detect_race_conditions=False)
    P = Prog()
    SP, AD, ACT, DVE, PE, GP = "SP", "AD", "ACT", "DVE", "PE", "GP"

    def dram_in(name, shape):
        return nc.dram_tensor(name, shape, F32, kind="ExternalInput")

    h_x = dram_in("x", [T, D])
    h_gw = dram_in("gw5", [E, 2, 128, DC, HID])
    h_lw = dram_in("lw", [E, HID, D])
    h_xpw = dram_in("xpw3", [128, DC, 192])
    h_dtpw = dram_in("dtpwT", [DTR, D])
    h_opw = dram_in("opwT", [D, D])
    h_alog = dram_in("alog3", [128, DC, NST])
    h_sml = dram_in("sml_d", [128, 5, DC])
    h_rw = dram_in("rw3", [128, DC, E])
    h_pes = dram_in("pes2", [128, E])

    h_out = nc.dram_tensor("out", [T, D], F32, kind="ExternalOutput")
    scratch_kind = "ExternalOutput" if debug else "Internal"
    h_attn = nc.dram_tensor("attn_scr", [DC, 128, T], F32, kind=scratch_kind)
    h_xfm = nc.dram_tensor("xfm_scr", [DC, 128, T], F32, kind=scratch_kind)
    h_h = nc.dram_tensor("h_scr", [2, KCN, 128, T], BF16, kind=scratch_kind)
    if debug:
        h_dxn = nc.dram_tensor("dbg_xn", [128, DC, T], BF16,
                               kind="ExternalOutput")
        h_dbc = nc.dram_tensor("dbg_bc", [128, T], BF16,
                               kind="ExternalOutput")
        h_ddl = nc.dram_tensor("dbg_dl", [DTR, T], BF16,
                               kind="ExternalOutput")
        h_dcmb = nc.dram_tensor("dbg_cmb", [8, T], BF16,
                                kind="ExternalOutput")
        h_ddtm = nc.dram_tensor("dbg_dtm", [NTB, 128, DC, TB], BF16,
                                kind="ExternalOutput")
        h_du = nc.dram_tensor("dbg_u", [NTB, 128, DC, TB], BF16,
                              kind="ExternalOutput")
        h_dy = nc.dram_tensor("dbg_y", [NTB, 128, DC, TB], F32,
                              kind="ExternalOutput")
        h_dybf = nc.dram_tensor("dbg_ybf", [NTB, 128, DC, TB], BF16,
                                kind="ExternalOutput")
        h_dcbc = nc.dram_tensor("dbg_cbc", [128, T], BF16,
                                kind="ExternalOutput")
        h_dcst = nc.dram_tensor("dbg_cst", [1, TB], F32,
                                kind="ExternalOutput")
        h_dg0 = nc.dram_tensor("dbg_g0", [128, FT], BF16,
                               kind="ExternalOutput")
        h_dwg = nc.dram_tensor("dbg_wg", [128, 2, DC, 128], BF16,
                               kind="ExternalOutput")
        h_draw = nc.dram_tensor("dbg_raw", [128, FT], F32,
                                kind="ExternalOutput")
        h_dg1 = nc.dram_tensor("dbg_g1", [128, FT], BF16,
                               kind="ExternalOutput")
        h_dbcf = nc.dram_tensor("dbg_bcf", [128, TB], F32,
                                kind="ExternalOutput")

    t_id = nc.inline_tensor(np.eye(128, dtype=np.float32), "id_f32")

    gctx = contextlib.ExitStack()

    def sb(name, shape, dtype):
        return gctx.enter_context(nc.sbuf_tensor(name, shape, dtype))

    xn_fm = sb("xn_fm", [128, DC, T], BF16)
    bc_sb = sb("bc_sb", [128, T], BF16)
    dl_sb = sb("dl_sb", [DTR, T], BF16)
    cmb_fm = sb("cmb_fm", [8, T], BF16)
    a_fm = sb("a_fm", [128, DC, NST], F32)
    dtpw_bf = sb("dtpw_bf", [DTR, D], BF16)
    xpw_bf = sb("xpw_bf", [128, DC, 192], BF16)
    rw_bf = sb("rw_bf", [128, DC, E], BF16)
    id_f = sb("id_f", [128, 128], F32)
    id_b = sb("id_b", [128, 128], BF16)
    ones_b = sb("ones_b", [1, 128], BF16)
    sml = sb("sml", [128, 5, DC], F32)
    dv_c = sml[:, 0, :]
    nb_c = sml[:, 1, :]
    nw_c = sml[:, 2, :]
    rsc_c = sml[:, 3, :]
    sts = sb("sts", [128, 48], F32)
    jnk = sb("jnk", [128, 64], F32)
    rr_all = sts[:, 32:48]
    pes_bc = sb("pes_bc", [128, 8], F32)
    cmb8 = sb("cmb8", [128, 8], BF16)

    ps = gctx.enter_context(nc.psum_tensor("ps_arena", [128, 4096], F32))

    def bank(b, cols=512, rows=128):
        return ps[0:rows, b * 512: b * 512 + cols]

    def fence(eng):
        if eng == "scalar":
            P.op(eng, lambda e: e.copy(jnk[:, 32:64], jnk[:, 0:32]))
            P.op(eng, lambda e: e.copy(jnk[:, 32:64], jnk[:, 0:32]))
        else:
            P.op(eng, lambda e: e.tensor_copy(jnk[:, 32:64], jnk[:, 0:32]))
            P.op(eng, lambda e: e.tensor_copy(jnk[:, 32:64], jnk[:, 0:32]))

    P.dma("sync", lambda e: e.dma_start(a_fm[:, :, :], h_alog.ap()))
    P.dma("sync", lambda e: e.dma_start(sml[:, :, :], h_sml.ap()))
    P.dma("sync", lambda e: e.dma_start(id_f[:, :], t_id.ap()))

    sctx = contextlib.ExitStack()
    dtpw_f = sctx.enter_context(nc.sbuf_tensor("dtpw_f", [DTR, D], F32))
    xpw_f = sctx.enter_context(nc.sbuf_tensor("xpw_f", [128, DC, 192], F32))
    rw_f = sctx.enter_context(nc.sbuf_tensor("rw_f", [128, DC, E], F32))
    P.dma("sync", lambda e: e.dma_start(dtpw_f[:, :], h_dtpw.ap()))
    P.dma("sync", lambda e: e.dma_start(xpw_f[:, :, :], h_xpw.ap()))
    P.dma("sync", lambda e: e.dma_start(rw_f[:, :, :], h_rw.ap()))
    w_pes = P.dma("sync", lambda e: e.dma_start(pes_bc[:, :], h_pes.ap()))
    WS = [(SP, w_pes)]

    P.op("vector", lambda e: e.memset(sts[:, 12:13], EPS_LN), waits=WS)
    P.op("vector", lambda e: e.memset(sts[:, 13:14], EPS_RMS), waits=WS)
    nc.const_aps.register(F32, EPS_LN, sts[:, 12:13])
    nc.const_aps.register(F32, EPS_RMS, sts[:, 13:14])
    P.op("vector", lambda e: e.memset(sts[:, 14:15], -3.4657359027997265),
         waits=WS)
    nc.const_aps.register(F32, -3.4657359027997265, sts[:, 14:15])

    c_ae = P.op("scalar", lambda e: e.activation(
        a_fm[:, :, :], a_fm[:, :, :], AF.Exp), waits=WS)
    P.op("vector", lambda e: e.tensor_scalar_mul(
        a_fm[:, :, :], a_fm[:, :, :], -1.0), waits=[(ACT, c_ae)])
    P.op("vector", lambda e: e.tensor_scalar_mul(nb_c, nb_c, -1.0), waits=WS)
    P.op("vector", lambda e: e.memset(ones_b[:, :], 1.0), waits=WS)
    for dc in range(DC):
        P.op("vector", lambda e, dc=dc: e.tensor_scalar_mul(
            rw_f[:, dc, :], rw_f[:, dc, :], rsc_c[:, dc:dc + 1]), waits=WS)
    c_vs = P.cnt[DVE]
    P.op("scalar", lambda e: e.copy(dtpw_bf[:, :], dtpw_f[:, :]), waits=WS)
    P.op("scalar", lambda e: e.copy(xpw_bf[:, :, :], xpw_f[:, :, :]), waits=WS)
    P.op("scalar", lambda e: e.copy(id_b[:, :], id_f[:, :]), waits=WS)
    c_setup_act = P.op("scalar", lambda e: e.copy(
        rw_bf[:, :, :], rw_f[:, :, :]), waits=[(DVE, c_vs)])
    a_neg_dve = P.cnt[DVE]

    p1 = contextlib.ExitStack()
    xt_r = [p1.enter_context(nc.sbuf_tensor(f"xt{k}", [128, D], F32))
            for k in range(3)]
    xq_r = [p1.enter_context(nc.sbuf_tensor(f"xq{k}", [128, D], F32))
            for k in range(2)]
    xnt_r = [p1.enter_context(nc.sbuf_tensor(f"xnt{k}", [128, D], BF16))
             for k in range(2)]
    xnm_r = [p1.enter_context(nc.sbuf_tensor(f"xnm{k}", [128, DC, 128], BF16))
             for k in range(2)]
    dgr_r = [p1.enter_context(nc.sbuf_tensor(f"dgr{k}", [128, 128], BF16))
             for k in range(2)]
    xf_r = [p1.enter_context(nc.sbuf_tensor(f"xf{k}", [128, DC, 128], F32))
            for k in range(2)]
    r8 = [p1.enter_context(nc.sbuf_tensor(f"r8{k}", [128, 8], F32))
          for k in range(6)]

    g_xt = Guard(3)
    g_xq = Guard(2)
    g_xnt = Guard(2)
    g_xnm = Guard(2)
    g_dgr = Guard(2)
    g_xf = Guard(2)
    g_tb01 = Guard(2)
    g_tb23 = Guard(2)
    g_b4 = Guard(1)
    g_b5 = Guard(1)
    g_b6 = Guard(1)
    g_b7 = Guard(1)

    mu, var_, istd, t1c, t2c, msc, m1c, nm1c, m2c, smc, rsv, rr32c = (
        sts[:, k:k + 1] for k in range(12))

    tbu = [0]
    for i in range(NTT):
        sl = slice(i * 128, (i + 1) * 128)
        xt, xq = xt_r[i % 3], xq_r[i % 2]
        xnt, xnm = xnt_r[i % 2], xnm_r[i % 2]
        dgr, xf = dgr_r[i % 2], xf_r[i % 2]

        w_x = P.dma("sync", lambda e, xt=xt, sl=sl: e.dma_start(
            xt[:, :], h_x.ap()[sl, :]), waits=g_xt.waits(i))
        WL = [(SP, w_x)]

        c_r1 = P.op("vector", lambda e, xt=xt: e.reduce_sum(
            t1c, xt[:, :], AX.X), waits=WL)
        c_sq = P.op("scalar", lambda e, xt=xt, xq=xq: e.activation(
            xq[:, :], xt[:, :], AF.Square), waits=WL + g_xq.waits(i))
        c_r2 = P.op("vector", lambda e, xq=xq: e.reduce_sum(
            t2c, xq[:, :], AX.X), waits=[(ACT, c_sq)])
        fence("vector")
        c_mu = P.op("vector", lambda e: e.tensor_scalar_mul(mu, t1c, 1.0 / D))
        fence("vector")
        P.op("vector", lambda e: e.tensor_scalar(t1c, mu, mu, -1.0,
                                                 ALU.mult, ALU.mult))
        P.op("vector", lambda e: e.tensor_scalar_mul(t2c, t2c, 1.0 / D))
        fence("vector")
        c_var = P.op("vector", lambda e: e.tensor_add(var_, t1c, t2c))
        P.op("scalar", lambda e: e.activation(var_, var_, AF.Ln, bias=EPS_LN),
             waits=[(DVE, c_var)])
        fence("scalar")
        c_istd = P.op("scalar", lambda e: e.activation(
            istd, var_, AF.Exp, scale=-0.5))
        c_xn = P.op("vector", lambda e, xnt=xnt, xt=xt: e.tensor_scalar(
            xnt[:, :], xt[:, :], mu, istd, ALU.subtract, ALU.mult),
            waits=[(ACT, c_istd)] + g_xnt.waits(i))
        c_sq2 = P.op("scalar", lambda e, xq=xq, xnt=xnt: e.activation(
            xq[:, :], xnt[:, :], AF.Square), waits=[(DVE, c_xn)])
        c_r3 = P.op("vector", lambda e, xq=xq: e.reduce_sum(
            msc, xq[:, :], AX.X), waits=[(ACT, c_sq2)])
        fence("vector")
        c_ms = P.op("vector", lambda e: e.tensor_scalar_mul(msc, msc, 1.0 / D))
        g_xq.set(i, (DVE, c_ms))
        P.op("scalar", lambda e: e.activation(msc, msc, AF.Ln, bias=EPS_RMS),
             waits=[(DVE, c_ms)])
        fence("scalar")
        rr = rr_all[:, i:i + 1]
        c_rr = P.op("scalar", lambda e, rr=rr: e.activation(
            rr, msc, AF.Exp, scale=-0.5))
        fence("scalar")
        c_rr32 = P.op("scalar", lambda e: e.activation(
            rr32c, msc, AF.Exp, scale=-0.5, bias=-3.4657359027997265))
        c_dg = P.op("vector", lambda e, dgr=dgr, rr=rr: e.tensor_scalar_mul(
            dgr[:, :], id_b[:, :], rr),
            waits=[(ACT, c_rr), (ACT, c_setup_act)] + g_dgr.waits(i))

        ev_last = 0
        for dc in range(DC):
            bidx = tbu[0]
            tbu[0] += 1
            c_mm = P.op("tensor", lambda e, xnt=xnt, dc=dc, b=bidx % 2:
                        e.matmul(bank(b, 128),
                                 xnt[:, dc * 128:(dc + 1) * 128], id_b[:, :],
                                 start=True, stop=True),
                        waits=[(DVE, c_xn), (ACT, c_setup_act)]
                        + g_tb01.waits(bidx))
            c_ev = P.op("scalar", lambda e, dc=dc, sl=sl, b=bidx % 2: e.copy(
                xn_fm[:, dc, sl], bank(b, 128)), waits=[(PE, c_mm)])
            g_tb01.set(bidx, (ACT, c_ev))
            ev_last = c_ev
        c_xn_ev = ev_last

        for dc in range(DC):
            bidx = tbu[0]
            tbu[0] += 1
            c_mm = P.op("tensor", lambda e, xnt=xnt, dgr=dgr, dc=dc,
                        b=bidx % 2: e.matmul(
                            bank(b, 128), xnt[:, dc * 128:(dc + 1) * 128],
                            dgr[:, :], start=True, stop=True),
                        waits=[(DVE, c_dg)] + g_tb01.waits(bidx))
            c_ev = P.op("scalar", lambda e, xnm=xnm, dc=dc, b=bidx % 2:
                        e.activation(xnm[:, dc, :], bank(b, 128), AF.Copy,
                                     scale=nw_c[:, dc:dc + 1]),
                        waits=[(PE, c_mm)] + g_xnm.waits(i))
            g_tb01.set(bidx, (ACT, c_ev))
        c_xnm_ev = P.cnt[ACT]
        g_xnt.set(i, (PE, P.cnt[PE]))
        g_dgr.set(i, (PE, P.cnt[PE]))

        for dc in range(DC):
            bidx = i * DC + dc
            c_mm = P.op("tensor", lambda e, xt=xt, dc=dc, b=bidx % 2:
                        e.matmul(bank(2 + b, 128),
                                 xt[:, dc * 128:(dc + 1) * 128], id_f[:, :],
                                 start=True, stop=True),
                        waits=WL + g_tb23.waits(bidx))
            c_cp = P.op("vector", lambda e, xf=xf, dc=dc, b=bidx % 2:
                        e.tensor_copy(xf[:, dc, :], bank(2 + b, 128)),
                        waits=[(PE, c_mm)] + (g_xf.waits(i) if dc == 0 else []))
            g_tb23.set(bidx, (DVE, c_cp))
        g_xt.set(i, (PE, P.cnt[PE]))
        c_xf = P.cnt[DVE]
        w_xf = P.dma("sync", lambda e, xf=xf, sl=sl: e.dma_start(
            bass.AP(h_xfm, sl.start, [[T, 128], [128 * T, DC], [1, 128]]),
            xf[:, :, :]), waits=[(DVE, c_xf)])
        g_xf.set(i, (SP, w_xf))

        for dc in range(DC):
            P.op("tensor", lambda e, xnm=xnm, dc=dc: e.matmul(
                bank(4, 128), xpw_bf[:, dc, 0:128], xnm[:, dc, :],
                start=(dc == 0), stop=(dc == DC - 1)),
                waits=([(ACT, c_xnm_ev)] + g_b4.waits(0)) if dc == 0 else [],
                inc=(1 if dc == DC - 1 else 0))
        c_p1m = P.cnt[PE]
        for dc in range(DC):
            P.op("tensor", lambda e, xnm=xnm, dc=dc: e.matmul(
                bank(5, 128, 64), xpw_bf[:, dc, 128:192], xnm[:, dc, :],
                start=(dc == 0), stop=(dc == DC - 1)),
                waits=g_b5.waits(0) if dc == 0 else [],
                inc=(1 if dc == DC - 1 else 0))
        c_p2m = P.cnt[PE]
        g_xnm.set(i, (PE, c_p2m))
        e1 = P.op("scalar", lambda e, sl=sl: e.copy(
            dl_sb[:, sl], ps[0:64, 4 * 512:4 * 512 + 128]),
            waits=[(PE, c_p1m)])
        e2 = P.op("scalar", lambda e, sl=sl: e.copy(
            bc_sb[64:128, sl], ps[64:128, 4 * 512:4 * 512 + 128]))
        g_b4.set(0, (ACT, e2))
        e3 = P.op("scalar", lambda e, sl=sl: e.copy(
            bc_sb[0:64, sl], ps[0:64, 5 * 512:5 * 512 + 128]),
            waits=[(PE, c_p2m)])
        g_b5.set(0, (ACT, e3))

        for dc in range(DC):
            P.op("tensor", lambda e, dc=dc, sl=sl: e.matmul(
                bank(6, 8), xn_fm[:, dc, sl], rw_bf[:, dc, :],
                start=(dc == 0), stop=(dc == DC - 1)),
                waits=([(ACT, c_xn_ev)] + g_b6.waits(0)) if dc == 0 else [],
                inc=(1 if dc == DC - 1 else 0))
        c_lg = P.cnt[PE]
        lgs, p8v, ge1, l2v, tmv, pmv = r8
        c0 = P.op("scalar", lambda e: e.activation(
            lgs[:, :], bank(6, 8), AF.Copy, scale=rr32c),
            waits=[(PE, c_lg), (ACT, c_rr32)])
        g_b6.set(0, (ACT, c0))
        c_m1 = P.op("vector", lambda e: e.reduce_max(m1c, lgs[:, :], AX.X),
                    waits=[(ACT, c0)])
        fence("vector")
        c_nm = P.op("vector", lambda e: e.tensor_scalar_mul(nm1c, m1c, -1.0))
        c_pa = P.op("scalar", lambda e: e.activation(
            p8v[:, :], lgs[:, :], AF.Exp, bias=nm1c), waits=[(DVE, c_nm)])
        c_g1 = P.op("vector", lambda e: e.tensor_single_scalar(
            ge1[:, :], lgs[:, :], m1c, ALU.is_ge))
        fence("vector")
        c_l2 = P.op("vector", lambda e: e.scalar_tensor_tensor(
            l2v[:, :], ge1[:, :], -BIGPOS, lgs[:, :], ALU.mult, ALU.add))
        fence("vector")
        c_m2 = P.op("vector", lambda e: e.reduce_max(m2c, l2v[:, :], AX.X))
        fence("vector")
        c_tm2 = P.op("vector", lambda e: e.tensor_single_scalar(
            tmv[:, :], lgs[:, :], m2c, ALU.is_ge))
        fence("vector")
        c_pm = P.op("vector", lambda e: e.tensor_mul(
            pmv[:, :], p8v[:, :], tmv[:, :]), waits=[(ACT, c_pa)])
        fence("vector")
        c_sm = P.op("vector", lambda e: e.reduce_sum(smc, pmv[:, :], AX.X))
        fence("vector")
        P.op("vector", lambda e: e.reciprocal(rsv, smc))
        fence("vector")
        P.op("vector", lambda e: e.tensor_scalar_mul(pmv[:, :], pmv[:, :], rsv))
        fence("vector")
        c_cm = P.op("vector", lambda e: e.tensor_mul(
            pmv[:, :], pmv[:, :], pes_bc[:, :]))
        c_cc = P.op("scalar", lambda e: e.copy(cmb8[:, :], pmv[:, :]),
                    waits=[(DVE, c_cm)] + g_b7.waits(0))
        c_tm = P.op("tensor", lambda e: e.matmul(
            ps[0:8, 7 * 512:7 * 512 + 128], cmb8[:, :], id_b[:, :],
            start=True, stop=True), waits=[(ACT, c_cc)] + g_b7.waits(0))
        c_ce = P.op("scalar", lambda e, sl=sl: e.copy(
            cmb_fm[0:8, sl], ps[0:8, 7 * 512:7 * 512 + 128]),
            waits=[(PE, c_tm)])
        g_b7.set(0, (ACT, c_ce), (PE, c_tm))

    p1_done = P.state()
    if debug:
        P.dma("sync", lambda e: e.dma_start(h_dxn.ap(), xn_fm[:, :, :]),
              waits=[(ACT, p1_done[ACT])])
        P.dma("sync", lambda e: e.dma_start(h_dbc.ap(), bc_sb[:, :]))
        P.dma("sync", lambda e: e.dma_start(h_ddl.ap(), dl_sb[:, :]))
        P.dma("sync", lambda e: e.dma_start(h_dcmb.ap(), cmb_fm[:, :]))
    p1.close()
    sctx.close()

    mx = contextlib.ExitStack()

    def mb(name, shape, dtype):
        return mx.enter_context(nc.sbuf_tensor(name, shape, dtype))

    dtm = mb("dtm", [128, DC, TB], BF16)
    u_bf = mb("u_bf", [128, DC, TB], BF16)
    y_acc = mb("y_acc", [128, DC, TB], F32)
    sg_r = [mb(f"sg{k}", [128, TB], F32) for k in range(2)]
    da_r = [mb(f"da{k}", [128, TB], F32) for k in range(2)]
    s_r = [mb(f"s{k}", [128, TB], BF16) for k in range(2)]
    hh_r = [mb(f"hm{k}", [128, TB], BF16) for k in range(2)]
    z_r = [mb(f"z{k}", [128, TB], BF16) for k in range(2)]
    bcB_r = [mb("bcB0", [128, TB], BF16)]
    bcC_r = [mb("bcC0", [128, TB], BF16)]
    stage = mb("stage", [1, 1, 2 * TB], BF16)
    rr_bc = mb("rr_bc", [128, T], BF16)
    rrT = mb("rrT", [16, 128], BF16)
    ybf = mb("ybf", [128, DC, TB], BF16)
    opw_bf = mb("opw_bf", [128, DC, D], BF16)
    opwf_r = [mb("opwf0", [128, D], F32)]
    xfr_r = [mb(f"xfr{k}", [128, FT], F32) for k in range(2)]
    evr_r = [mb(f"evr{k}", [128, FT], F32) for k in range(2)]

    def bcb(which, half):
        b = 2 + 2 * which + half
        return ps[:, b * 512:(b + 1) * 512]

    g_dt = Guard(2)
    g_bcB = Guard(1)
    g_bcC = Guard(1)
    g_sbB = Guard(1)
    g_sbC = Guard(1)
    g_stage = Guard(1)
    g_da = Guard(2)
    g_s = Guard(2)
    g_hm = Guard(2)
    g_z = Guard(2)
    g_sg = Guard(2)
    g_op67 = Guard(2)
    g_xfr = Guard(2)
    g_evr = Guard(2)
    g_opwf = Guard(1)

    opw_c = 0
    for dci in range(DC):
        w = P.dma("sync", lambda e, dci=dci: e.dma_start(
            opwf_r[0][:, :], h_opw.ap()[dci * 128:(dci + 1) * 128, :]),
            waits=g_opwf.waits(dci))
        opw_c = P.op("scalar", lambda e, dci=dci: e.copy(
            opw_bf[:, dci, :], opwf_r[0][:, :]), waits=[(SP, w)])
        g_opwf.set(dci, (ACT, opw_c))

    c_rc = P.op("scalar", lambda e: e.copy(jnk[:, 0:16].bitcast(BF16)[:, 0:16],
                                           rr_all),
                waits=[(ACT, p1_done[ACT]), (DVE, p1_done[DVE])])
    c_rt = P.op("tensor", lambda e: e.matmul(
        bank(0, 128, 16), jnk[:, 0:16].bitcast(BF16)[:, 0:16], id_b[:, :],
        start=True, stop=True), waits=[(ACT, c_rc)])
    c_re = P.op("scalar", lambda e: e.copy(rrT[0:16, :], bank(0, 128, 16)),
                waits=[(PE, c_rt)])
    rseq = stage[0:1, 0, :]
    w_rf = P.dma("sync", lambda e: e.dma_start(rseq, rrT[0:16, :]),
                 waits=[(ACT, c_re)])
    for q4 in range(4):
        c_rb = P.op("tensor", lambda e, q4=q4: e.matmul(
            bank(1, 512), ones_b[0:1, :],
            stage[0:1, 0, q4 * 512:(q4 + 1) * 512], start=True, stop=True),
            waits=[(SP, w_rf)] + ([(ACT, P.cnt["ACT"])] if q4 else []))
        P.op("scalar", lambda e, q4=q4: e.copy(
            rr_bc[:, q4 * 512:(q4 + 1) * 512], bank(1, 512)),
            waits=[(PE, c_rb)])
    c_rrbc = P.cnt["ACT"]

    g_stage.set(0, (PE, P.cnt[PE]))
    g_dt.set(0, (ACT, c_rrbc))
    g_dt.set(1, (ACT, c_rrbc))
    xnm_t = s_r[0]
    uidx = 0
    for tb in range(NTB):
        t0 = tb * TB
        tsl = slice(t0, t0 + TB)
        snap = P.state()

        for dc in range(DC):
            sg = sg_r[dc % 2]
            for half in range(2):
                hsl = slice(t0 + half * 512, t0 + (half + 1) * 512)
                c_mm = P.op("tensor", lambda e, dc=dc, hsl=hsl, half=half:
                            e.matmul(bank(half, 512),
                                     dtpw_bf[:, dc * 128:(dc + 1) * 128],
                                     dl_sb[:, hsl], start=True, stop=True),
                            waits=[(ACT, p1_done[ACT])] + g_dt.waits(half))
                c_sg = P.op("scalar", lambda e, sg=sg, dc=dc, half=half:
                            e.activation(sg[:, half * 512:(half + 1) * 512],
                                         bank(half, 512), AF.Sigmoid,
                                         scale=-1.0, bias=nb_c[:, dc:dc + 1]),
                            waits=[(PE, c_mm)] + (g_sg.waits(dc)
                                                  if half == 0 else []))
                g_dt.set(half, (ACT, c_sg))
            c_ln = P.op("scalar", lambda e, sg=sg: e.activation(
                sg[:, :], sg[:, :], AF.Ln))
            c_d1 = P.op("vector", lambda e, sg=sg, dc=dc: e.tensor_scalar_mul(
                bass.AP(dtm, dc * TB + 1,
                        [[DC * TB, 128], [CH, TB // CH], [1, CH - 1]]),
                bass.AP(sg, 1, [[TB, 128], [CH, TB // CH], [1, CH - 1]]),
                -1.0),
                waits=[(ACT, c_ln), (ACT, snap[ACT]), (GP, snap[GP])])
            c_d2 = P.op("vector", lambda e, dc=dc: e.memset(
                bass.AP(dtm, dc * TB, [[DC * TB, 128], [CH, TB // CH], [1, 1]]),
                BIGPOS))
            c_dx = P.op("vector", lambda e, dc=dc, tsl=tsl:
                        e.scalar_tensor_tensor(
                            xnm_t[:, :], xn_fm[:, dc, tsl], nw_c[:, dc:dc + 1],
                            rr_bc[:, tsl], ALU.mult, ALU.mult),
                        waits=[(ACT, c_rrbc)])
            c_d3 = P.op("vector", lambda e, sg=sg, dc=dc:
                        e.scalar_tensor_tensor(
                            u_bf[:, dc, :], sg[:, :], -1.0, xnm_t[:, :],
                            ALU.mult, ALU.mult))
            g_sg.set(dc, (DVE, c_d3))
        c_p2d = P.cnt[DVE]

        c_y0 = P.op("vector", lambda e: e.memset(y_acc[:, :, :], 0.0),
                    waits=[(GP, P.cnt[GP])])

        for n in range(NST):
            st = stage[0:1, 0, :]
            w_st = P.dma("sync", lambda e, st=st, n=n, t0=t0: e.dma_start(
                st, bass.AP(bc_sb, n * T + t0, [[64 * T, 2], [1, TB]])),
                waits=g_stage.waits(n) + [(ACT, p1_done[ACT])])
            for half in range(2):
                P.op("tensor", lambda e, half=half: e.matmul(
                    bcb(0, half), ones_b[0:1, :],
                    stage[0:1, 0, TB + half * 512:TB + (half + 1) * 512],
                    start=True, stop=True),
                    waits=([(SP, w_st)] + g_bcB.waits(0)) if half == 0 else [],
                    inc=(1 if half == 1 else 0))
            c_mB = P.cnt[PE]
            for half in range(2):
                P.op("tensor", lambda e, half=half: e.matmul(
                    bcb(1, half), ones_b[0:1, :],
                    stage[0:1, 0, half * 512:(half + 1) * 512],
                    start=True, stop=True),
                    waits=g_bcC.waits(0) if half == 0 else [],
                    inc=(1 if half == 1 else 0))
            c_mC = P.cnt[PE]
            g_stage.set(n, (PE, c_mC))
            bB = bcB_r[0]
            bC = bcC_r[0]
            c_eB = P.op("scalar", lambda e, bB=bB: e.copy(
                bB[:, :], ps[:, 2 * 512:4 * 512]),
                waits=[(PE, c_mB)] + g_sbB.waits(n))
            g_bcB.set(0, (ACT, c_eB))
            c_eC = P.op("scalar", lambda e, bC=bC: e.copy(
                bC[:, :], ps[:, 4 * 512:6 * 512]),
                waits=[(PE, c_mC)] + g_sbC.waits(n))
            g_bcC.set(0, (ACT, c_eC))

            sB_cons = []
            sC_cons = []
            for dc in range(DC):
                da = da_r[uidx % 2]
                sb_ = s_r[uidx % 2]
                hb = hh_r[uidx % 2]
                zb = z_r[uidx % 2]
                c_exp = P.op("scalar", lambda e, da=da, dc=dc, n=n:
                             e.activation(da[:, :], dtm[:, dc, :], AF.Exp,
                                          scale=a_fm[:, dc, n:n + 1]),
                             waits=[(DVE, c_p2d)] + g_da.waits(uidx))
                c_s = P.op("vector", lambda e, sb_=sb_, dc=dc, bB=bB:
                           e.tensor_mul(sb_[:, :], u_bf[:, dc, :], bB[:, :]),
                           waits=[(ACT, c_eB), (DVE, c_p2d)] + g_s.waits(uidx))
                sB_cons.append((DVE, c_s))
                c_scan = P.op("vector", lambda e, hb=hb, da=da, sb_=sb_:
                              e.tensor_tensor_scan(hb[:, :], da[:, :],
                                                   sb_[:, :], 0.0,
                                                   ALU.mult, ALU.add),
                              waits=[(ACT, c_exp)] + g_hm.waits(uidx))
                g_da.set(uidx, (DVE, c_scan))
                g_s.set(uidx, (DVE, c_scan))
                c_z = P.op("vector", lambda e, zb=zb, hb=hb, bC=bC:
                           e.tensor_mul(zb[:, :], hb[:, :], bC[:, :]),
                           waits=[(ACT, c_eC)] + g_z.waits(uidx))
                zdep = (DVE, c_z)
                sC_cons.append(zdep)
                g_hm.set(uidx, zdep)
                c_y = P.op("vector", lambda e, dc=dc, zb=zb: e.tensor_add(
                    y_acc[:, dc, :], y_acc[:, dc, :], zb[:, :]),
                    waits=[(DVE, c_y0)])
                g_z.set(uidx, (DVE, c_y))
                uidx += 1
            g_sbB.set(n, *sB_cons)
            g_sbC.set(n, *sC_cons)

        if debug:
            P.dma("sync", lambda e, tb=tb: e.dma_start(
                h_ddtm.ap()[tb], dtm[:, :, :]), waits=[(DVE, c_p2d)])
            P.dma("sync", lambda e, tb=tb: e.dma_start(
                h_du.ap()[tb], u_bf[:, :, :]))
            P.dma("sync", lambda e, tb=tb: e.dma_start(
                h_dy.ap()[tb], y_acc[:, :, :]), waits=[(GP, P.cnt[GP])])

        c_gpd = P.cnt[GP]
        for dc in range(DC):
            P.op("vector", lambda e, dc=dc, tsl=tsl: e.scalar_tensor_tensor(
                ybf[:, dc, :], xn_fm[:, dc, tsl], dv_c[:, dc:dc + 1],
                y_acc[:, dc, :], ALU.mult, ALU.add),
                waits=[(GP, c_gpd), (PE, snap[PE])])
        c_ybf = P.cnt[DVE]
        if debug:
            P.dma("sync", lambda e, tb=tb: e.dma_start(
                h_dybf.ap()[tb], ybf[:, :, :]), waits=[(DVE, c_ybf)])
        for q in range(2):
            qsl = slice(t0 + q * 512, t0 + (q + 1) * 512)
            for dco in range(DC):
                it = q * DC + dco
                par = it % 2
                w_xf = P.dma("sync", lambda e, dco=dco, qsl=qsl, par=par:
                             e.dma_start(xfr_r[par][:, :],
                                         h_xfm.ap()[dco, :, qsl]),
                             waits=[(SP, p1_done[SP])] + g_xfr.waits(it))
                for dci in range(DC):
                    P.op("tensor", lambda e, dci=dci, dco=dco, q=q, par=par:
                         e.matmul(bank(6 + par, 512),
                                  opw_bf[:, dci, dco * 128:(dco + 1) * 128],
                                  ybf[:, dci, q * 512:(q + 1) * 512],
                                  start=(dci == 0), stop=(dci == DC - 1)),
                         waits=([(DVE, c_ybf), (ACT, opw_c)]
                                + g_op67.waits(it)) if dci == 0 else [],
                         inc=(1 if dci == DC - 1 else 0))
                c_op = P.cnt[PE]
                c_ev = P.op("vector", lambda e, par=par: e.tensor_add(
                    evr_r[par][:, :], bank(6 + par, 512), xfr_r[par][:, :]),
                    waits=[(PE, c_op), (SP, w_xf)] + g_evr.waits(it))
                g_op67.set(it, (DVE, c_ev))
                g_xfr.set(it, (DVE, c_ev))
                w_at = P.dma("sync", lambda e, dco=dco, qsl=qsl, par=par:
                             e.dma_start(h_attn.ap()[dco, :, qsl],
                                         evr_r[par][:, :]),
                             waits=[(DVE, c_ev)])
                g_evr.set(it, (SP, w_at))

    mixer_done = P.state()
    mx.close()

    fx = contextlib.ExitStack()

    def fb(name, shape, dtype):
        return fx.enter_context(nc.sbuf_tensor(name, shape, dtype))

    acc = fb("acc", [128, DC, T], F32)
    ffx = contextlib.ExitStack()

    def fb2(name, shape, dtype):
        return ffx.enter_context(nc.sbuf_tensor(name, shape, dtype))

    wgf_r = [fb2(f"wgf{k}", [128, 2, DC, 128], F32) for k in range(2)]
    wgb_r = [fb2(f"wgb{k}", [128, 2, DC, 128], BF16) for k in range(2)]
    wlf_r = [fb2("wlf0", [128, D], F32)]
    wl_e = fb2("wl_e", [128, KCN, D], BF16)
    g0s_r = [fb2(f"g0s{k}", [128, FT], BF16) for k in range(2)]
    g1s_r = [fb2(f"g1s{k}", [128, FT], BF16) for k in range(2)]
    t1_r = [fb2(f"t1{k}", [128, FT], BF16) for k in range(2)]
    hup_r = [fb2(f"hup{k}", [128, FT], BF16) for k in range(2)]
    hin_r = [fb2(f"hin{k}", [128, FT], BF16) for k in range(2)]
    cmb_bc = fb2("cmb_bc", [128, T], BF16)
    cstage = fb2("cstage", [1, TB], BF16)
    jnk2 = fb2("jnk2", [128, FT], F32) if True else None

    g_wgf = Guard(2)
    g_wgb = Guard(2)
    g_wlf = Guard(1)
    g_wle = Guard(1)
    g_gb = Guard(8)
    g_g0s = Guard(2)
    g_g1s = Guard(2)
    g_t1 = Guard(2)
    g_hup = Guard(2)
    g_hin = Guard(2)
    g_cst = Guard(1)
    g_cbc = Guard(1)
    g_dn = Guard(1)

    acc_w = []
    for dc in range(DC):
        w = P.dma("sync", lambda e, dc=dc: e.dma_start(
            acc[:, dc, :], h_attn.ap()[dc, :, :]),
            waits=[(DVE, mixer_done[DVE]), (SP, mixer_done[SP])])
        acc_w.append(w)

    hidx = 0
    hw_at = {}
    for e_i in range(E):
        c_cbc = 0
        for hf in range(2):
            w_cs = P.dma("sync", lambda e, e_i=e_i, hf=hf: e.dma_start(
                cstage[0:1, :], cmb_fm[e_i:e_i + 1, hf * TB:(hf + 1) * TB]),
                waits=[(ACT, p1_done[ACT]), (ACT, c_cbc)] + g_cst.waits(0))
            c_pe_cs = 0
            for q4 in range(2):
                c_bcm = P.op("tensor", lambda e, q4=q4: e.matmul(
                    bank(0, 512), ones_b[0:1, :],
                    cstage[0:1, q4 * 512:(q4 + 1) * 512],
                    start=True, stop=True),
                    waits=[(SP, w_cs), (ACT, c_cbc)] + g_gb.waits(0))
                c_cbc = P.op("scalar", lambda e, hf=hf, q4=q4: e.copy(
                    cmb_bc[:, hf * TB + q4 * 512:hf * TB + (q4 + 1) * 512],
                    bank(0, 512)), waits=[(PE, c_bcm)] + g_cbc.waits(0))
                g_gb.set(0, (ACT, c_cbc))
                c_pe_cs = c_bcm
            g_cst.set(0, (PE, c_pe_cs))

        if debug and e_i == 0:
            w_dbg = P.dma("sync", lambda e: e.dma_start(
                h_dcbc.ap(), cmb_bc[:, :]),
                waits=[(ACT, P.cnt["ACT"])])
            g_cbc.set(0, (SP, w_dbg))
        up_dve_last = 0
        for kc in range(KCN):
            par = kc % 2
            w1 = P.dma("sync", lambda e, e_i=e_i, kc=kc, par=par: e.dma_start(
                wgf_r[par][:, 0, :, :],
                h_gw.ap()[e_i, 0, :, :, kc * 128:(kc + 1) * 128]),
                waits=g_wgf.waits(kc))
            w2 = P.dma("sync", lambda e, e_i=e_i, kc=kc, par=par: e.dma_start(
                wgf_r[par][:, 1, :, :],
                h_gw.ap()[e_i, 1, :, :, kc * 128:(kc + 1) * 128]))
            w3 = P.dma("sync", lambda e, e_i=e_i, kc=kc, par=par: e.dma_start(
                wlf_r[0][:, :], h_lw.ap()[e_i, kc * 128:(kc + 1) * 128, :]),
                waits=g_wlf.waits(kc))
            cg = P.op("gpsimd", lambda e, par=par: e.tensor_copy(
                wgb_r[par][:, :, :, :], wgf_r[par][:, :, :, :]),
                waits=[(SP, w2)] + g_wgb.waits(kc))
            g_wgf.set(kc, (GP, cg))
            cl = P.op("gpsimd", lambda e, kc=kc: e.tensor_copy(
                wl_e[:, kc, :], wlf_r[0][:, :]),
                waits=[(SP, w3)] + (g_wle.waits(0) if kc == 0 else []))
            if debug and e_i == 0 and kc == 0:
                P.dma("sync", lambda e: e.dma_start(
                    h_dwg.ap(), wgb_r[0][:, :, :, :]), waits=[(GP, cg)])
            g_wlf.set(kc, (GP, cl))

            bank_done = {}
            for g in range(2):
                for dck in range(DC):
                    for ti4 in range(NFT):
                        b = g * 4 + ti4
                        c = P.op("tensor", lambda e, g=g, dck=dck, par=par,
                                 ti4=ti4, b=b: e.matmul(
                                     bank(b, 512), wgb_r[par][:, g, dck, :],
                                     xn_fm[:, dck, ti4 * FT:(ti4 + 1) * FT],
                                     start=(dck == 0), stop=(dck == DC - 1)),
                                 waits=([(GP, cg)] + g_gb.waits(b))
                                 if (dck == 0 and ti4 == 0) else
                                 (g_gb.waits(b) if dck == 0 else []),
                                 inc=(1 if dck == DC - 1 else 0))
                        if dck == DC - 1:
                            bank_done[b] = c
            for ti in range(NFT):
                fsl = slice(ti * FT, (ti + 1) * FT)
                g0s = g0s_r[ti % 2]
                g1s = g1s_r[ti % 2]
                t1b = t1_r[ti % 2]
                hup = hup_r[hidx % 2]
                c_e0 = P.op("scalar", lambda e, g0s=g0s, b=ti:
                            e.activation(g0s[:, :], bank(b, 512),
                                         AF.Gelu_apprx_tanh),
                            waits=[(PE, bank_done[ti])] + g_g0s.waits(ti))
                g_gb.set(ti, (ACT, c_e0))
                c_e1 = P.op("scalar", lambda e, g1s=g1s, b=4 + ti:
                            e.activation(g1s[:, :], bank(b, 512), AF.Copy),
                            waits=[(PE, bank_done[4 + ti])] + g_g1s.waits(ti))
                g_gb.set(4 + ti, (ACT, c_e1))
                if debug and e_i == 0 and kc == 0 and ti == 0:
                    P.dma("sync", lambda e: e.dma_start(
                        h_dg0.ap(), g0s[:, :]), waits=[(ACT, c_e0)])
                    P.dma("sync", lambda e: e.dma_start(
                        h_dg1.ap(), g1s[:, :]), waits=[(ACT, c_e1)])
                c_t1 = P.op("vector", lambda e, t1b=t1b, g1s=g1s, fsl=fsl:
                            e.tensor_mul(t1b[:, :], g1s[:, :], cmb_bc[:, fsl]),
                            waits=[(ACT, c_e1), (ACT, c_cbc)] + g_t1.waits(ti))
                g_g1s.set(ti, (DVE, c_t1))
                c_h = P.op("vector", lambda e, hup=hup, g0s=g0s, t1b=t1b:
                           e.tensor_mul(hup[:, :], g0s[:, :], t1b[:, :]),
                           waits=[(ACT, c_e0)] + g_hup.waits(hidx))
                g_g0s.set(ti, (DVE, c_h),
                          (("SP", P.cnt["SP"]) if (debug and e_i == 0
                           and kc == 0 and ti == 0) else None))
                g_t1.set(ti, (DVE, c_h))
                w_h = P.dma("scalar", lambda e, e_i=e_i, kc=kc, fsl=fsl,
                            hup=hup: e.dma_start(
                                h_h.ap()[e_i % 2, kc, :, fsl], hup[:, :]),
                            waits=[(DVE, c_h)])
                hw_at[(kc, ti)] = w_h
                g_hup.set(hidx, (AD, w_h))
                hidx += 1
                up_dve_last = c_h
            g_wgb.set(kc, (PE, P.cnt[PE]))
        c_up_act = P.cnt[ACT]
        c_up_ad = P.cnt[AD]
        g_cbc.set(0, (DVE, up_dve_last))

        for ti in range(NFT):
            fsl = slice(ti * FT, (ti + 1) * FT)
            for kc in range(KCN):
                hin = hin_r[(ti * KCN + kc) % 2]
                w_hi = P.dma("scalar", lambda e, e_i=e_i, kc=kc, fsl=fsl,
                             hin=hin: e.dma_start(
                                 hin[:, :], h_h.ap()[e_i % 2, kc, :, fsl]),
                             waits=[(AD, hw_at[(kc, ti)])]
                             + g_hin.waits(ti * KCN + kc))
                for dcm in range(DC):
                    P.op("tensor", lambda e, kc=kc, dcm=dcm, hin=hin:
                         e.matmul(
                             ps[:, dcm * 512:(dcm + 1) * 512],
                             wl_e[:, kc, dcm * 128:(dcm + 1) * 128],
                             hin[:, :],
                             start=(kc == 0), stop=(kc == KCN - 1),
                             skip_group_check=True),
                         waits=([(AD, w_hi), (GP, P.cnt[GP]),
                                 (ACT, c_up_act)] + g_dn.waits(0))
                         if (dcm == 0 and kc == 0) else
                         ([(AD, w_hi)] if dcm == 0 else []),
                         inc=(1 if dcm == DC - 1 else 0))
                g_hin.set(ti * KCN + kc, (PE, P.cnt[PE]))
            c_dn = P.cnt[PE]
            evs = []
            for dcm in range(DC):
                c_ev = P.op("vector", lambda e, dcm=dcm, ti=ti:
                            e.tensor_tensor(
                                acc[:, dcm, ti * FT:(ti + 1) * FT],
                                ps[:, dcm * 512:(dcm + 1) * 512],
                                acc[:, dcm, ti * FT:(ti + 1) * FT],
                                ALU.add),
                            waits=[(PE, c_dn), (SP, acc_w[dcm])])
                evs.append((DVE, c_ev))
            g_dn.set(0, *evs)
            for b in range(8):
                g_gb.set(b, *evs)
        g_wle.set(0, (PE, P.cnt[PE]))

    ffn_done = P.state()
    ffx.close()

    outt_r = [fx.enter_context(
        nc.sbuf_tensor(f"outt{k}", [128, DC, 128], F32)) for k in range(2)]
    g_ot = Guard(2)
    g_tb = Guard(2)
    for i in range(NTT):
        sl = slice(i * 128, (i + 1) * 128)
        ot = outt_r[i % 2]
        evl = 0
        for dc in range(DC):
            bidx = i * DC + dc
            c_mm = P.op("tensor", lambda e, dc=dc, sl=sl, b=bidx % 2:
                        e.matmul(bank(b, 128), acc[:, dc, sl], id_f[:, :],
                                 start=True, stop=True),
                        waits=[(DVE, ffn_done[DVE])] + g_tb.waits(bidx))
            c_ev = P.op("scalar", lambda e, ot=ot, dc=dc, b=bidx % 2: e.copy(
                ot[:, dc, :], bank(b, 128)),
                waits=[(PE, c_mm)] + (g_ot.waits(i) if dc == 0 else []))
            g_tb.set(bidx, (ACT, c_ev))
            evl = c_ev
        w_o = P.dma("sync", lambda e, ot=ot, sl=sl: e.dma_start(
            bass.AP(h_out, sl.start * D, [[D, 128], [128, DC], [1, 128]]),
            ot[:, :, :]), waits=[(ACT, evl)])
        g_ot.set(i, (SP, w_o))

    ctx = P.emit(nc)
    nc._keepalive = (gctx, fx, ctx)
    return nc


def kernel(L0_attn_x, L0_attn_norm_w, L0_attn_x_proj_w, L0_attn_dt_proj_w,
           L0_attn_dt_proj_b, L0_attn_A_log, L0_attn_D, L0_attn_out_proj_w,
           L0_ffn_router_scale, L0_ffn_router_w, L0_ffn_gating_w,
           L0_ffn_linear_w, L0_ffn_per_expert_scale):
    f32 = np.float32
    x = np.ascontiguousarray(np.asarray(L0_attn_x, f32))
    gw = np.asarray(L0_ffn_gating_w, f32).transpose(0, 1, 3, 2)
    gw5 = np.ascontiguousarray(
        gw.reshape(E, 2, DC, 128, HID).transpose(0, 1, 3, 2, 4))
    lw = np.ascontiguousarray(np.asarray(L0_ffn_linear_w, f32))
    xpw3 = np.ascontiguousarray(
        np.asarray(L0_attn_x_proj_w, f32).T.reshape(DC, 128, 192)
        .transpose(1, 0, 2))
    dtpwT = np.ascontiguousarray(np.asarray(L0_attn_dt_proj_w, f32).T)
    opwT = np.ascontiguousarray(np.asarray(L0_attn_out_proj_w, f32).T)
    alog3 = np.ascontiguousarray(
        np.asarray(L0_attn_A_log, f32).reshape(DC, 128, NST).transpose(1, 0, 2))

    def col(v):
        return np.asarray(v, f32).reshape(DC, 128).T

    sml = np.ascontiguousarray(np.stack(
        [col(L0_attn_D), col(L0_attn_dt_proj_b), col(L0_attn_norm_w),
         col(L0_ffn_router_scale), np.zeros((128, DC), f32)], axis=1))
    rw3 = np.ascontiguousarray(
        np.asarray(L0_ffn_router_w, f32).reshape(DC, 128, E).transpose(1, 0, 2))
    pes2 = np.ascontiguousarray(np.broadcast_to(
        np.asarray(L0_ffn_per_expert_scale, f32).reshape(1, E), (128, E)))

    nc = build_nc(debug=DEBUG)
    in_maps = []
    for c in range(NCORES):
        in_maps.append(dict(
            x=np.ascontiguousarray(x[c]), gw5=gw5, lw=lw, xpw3=xpw3,
            dtpwT=dtpwT, opwT=opwT, alog3=alog3, sml_d=sml, rw3=rw3, pes2=pes2))
    res = run_bass_kernel_spmd(nc, in_maps, core_ids=list(range(NCORES)))
    out = np.stack([res.results[c]["out"] for c in range(NCORES)], axis=0)
    return out


# version 13
# speedup vs baseline: 1.1836x; 1.1836x over previous
import contextlib
import numpy as np
import concourse.bass as bass
import concourse.mybir as mybir
from concourse.bass_utils import run_bass_kernel_spmd

F32 = mybir.dt.float32
BF16 = mybir.dt.bfloat16
AF = mybir.ActivationFunctionType
ALU = mybir.AluOpType
AX = mybir.AxisListType

NCORES = 8
D = 1024
DC = 8
T = 2048
NTT = 16
NST = 64
CH = 64
TB = 1024
NTB = T // TB
HID = 2048
KCN = 16
E = 8
FT = 512
NFT = T // FT
DTR = 64
BIGPOS = 1.0e30
EPS_LN = 1e-5
EPS_RMS = 1e-6
Z_ON_GP_MOD = 1000000


class Prog:
    ENGINES = ("sync", "scalar", "vector", "tensor", "gpsimd")
    SEM_OF = {"sync": "SP", "scalar": "ACT", "vector": "DVE",
              "tensor": "PE", "gpsimd": "GP"}
    DMA_SEM_OF = {"sync": "SP", "scalar": "AD", "gpsimd": "GPD"}
    DMA_SEMS = ("SP", "AD", "GPD")

    def __init__(self):
        self.items = {e: [] for e in self.ENGINES}
        self.cnt = {s: 0 for s in ("SP", "ACT", "DVE", "PE", "GP", "AD", "GPD")}

    def op(self, eng, fn, waits=(), inc=1):
        sem = self.SEM_OF[eng]
        if inc:
            self.cnt[sem] += 1
        self.items[eng].append((fn, [w for w in waits if w and w[1] > 0],
                                sem if inc else None))
        return self.cnt[sem]

    def dma(self, eng, fn, waits=()):
        sem = self.DMA_SEM_OF[eng]
        self.cnt[sem] += 16
        self.items[eng].append((fn, [w for w in waits if w and w[1] > 0], sem))
        return self.cnt[sem]

    def state(self):
        return dict(self.cnt)

    def emit(self, nc):
        sems = {}
        ctx = contextlib.ExitStack()
        for name in self.cnt:
            sems[name] = ctx.enter_context(nc.semaphore("sem_" + name))
        final_waits = [(s, v) for s, v in self.cnt.items() if v > 0]

        def run(eng_name, e):
            own = self.SEM_OF[eng_name]
            last = {}
            for fn, waits, sem in self.items[eng_name]:
                for (ws, wv) in waits:
                    if ws == own and ws not in self.DMA_SEMS:
                        continue
                    if last.get(ws, -1) >= wv:
                        continue
                    e.wait_ge(sems[ws], wv)
                    last[ws] = wv
                inst = fn(e)
                if sem is not None:
                    inst.then_inc(sems[sem], 16 if sem in self.DMA_SEMS else 1)
            for ws, wv in final_waits:
                if ws == own and ws not in self.DMA_SEMS:
                    continue
                if last.get(ws, -1) >= wv:
                    continue
                e.wait_ge(sems[ws], wv)
                last[ws] = wv

        with nc.Block() as block:
            @block.sync
            def _(e):
                run("sync", e)

            @block.scalar
            def _(e):
                run("scalar", e)

            @block.vector
            def _(e):
                run("vector", e)

            @block.tensor
            def _(e):
                run("tensor", e)

            @block.gpsimd
            def _(e):
                run("gpsimd", e)
        return ctx


class Guard:

    def __init__(self, nslots):
        self.n = nslots
        self.v = [[] for _ in range(nslots)]

    def waits(self, i):
        return list(self.v[i % self.n])

    def set(self, i, *deps):
        self.v[i % self.n] = [d for d in deps if d and d[1] > 0]


DEBUG = False


def build_nc(debug=False):
    nc = bass.Bass("TRN2", target_bir_lowering=False,
                   detect_race_conditions=False)
    P = Prog()
    SP, AD, ACT, DVE, PE, GP = "SP", "AD", "ACT", "DVE", "PE", "GP"

    def dram_in(name, shape):
        return nc.dram_tensor(name, shape, F32, kind="ExternalInput")

    h_x = dram_in("x", [T, D])
    h_gw = dram_in("gw5", [E, 2, 128, DC, HID])
    h_lw = dram_in("lw", [E, HID, D])
    h_xpw = dram_in("xpw3", [128, DC, 192])
    h_dtpw = dram_in("dtpwT", [DTR, D])
    h_opw = dram_in("opwT", [D, D])
    h_alog = dram_in("alog3", [128, DC, NST])
    h_sml = dram_in("sml_d", [128, 5, DC])
    h_rw = dram_in("rw3", [128, DC, E])
    h_pes = dram_in("pes2", [128, E])

    h_out = nc.dram_tensor("out", [T, D], F32, kind="ExternalOutput")
    scratch_kind = "ExternalOutput" if debug else "Internal"
    h_attn = nc.dram_tensor("attn_scr", [DC, 128, T], F32, kind=scratch_kind)
    h_xfm = nc.dram_tensor("xfm_scr", [DC, 128, T], F32, kind=scratch_kind)
    h_h = nc.dram_tensor("h_scr", [2, KCN, 128, T], BF16, kind=scratch_kind)
    if debug:
        h_dxn = nc.dram_tensor("dbg_xn", [128, DC, T], BF16,
                               kind="ExternalOutput")
        h_dbc = nc.dram_tensor("dbg_bc", [128, T], BF16,
                               kind="ExternalOutput")
        h_ddl = nc.dram_tensor("dbg_dl", [DTR, T], BF16,
                               kind="ExternalOutput")
        h_dcmb = nc.dram_tensor("dbg_cmb", [8, T], BF16,
                                kind="ExternalOutput")
        h_ddtm = nc.dram_tensor("dbg_dtm", [NTB, 128, DC, TB], BF16,
                                kind="ExternalOutput")
        h_du = nc.dram_tensor("dbg_u", [NTB, 128, DC, TB], BF16,
                              kind="ExternalOutput")
        h_dy = nc.dram_tensor("dbg_y", [NTB, 128, DC, TB], F32,
                              kind="ExternalOutput")
        h_dybf = nc.dram_tensor("dbg_ybf", [NTB, 128, DC, TB], BF16,
                                kind="ExternalOutput")
        h_dcbc = nc.dram_tensor("dbg_cbc", [128, T], BF16,
                                kind="ExternalOutput")
        h_dcst = nc.dram_tensor("dbg_cst", [1, TB], F32,
                                kind="ExternalOutput")
        h_dg0 = nc.dram_tensor("dbg_g0", [128, FT], BF16,
                               kind="ExternalOutput")
        h_dwg = nc.dram_tensor("dbg_wg", [128, 2, DC, 128], BF16,
                               kind="ExternalOutput")
        h_draw = nc.dram_tensor("dbg_raw", [128, FT], F32,
                                kind="ExternalOutput")
        h_dg1 = nc.dram_tensor("dbg_g1", [128, FT], BF16,
                               kind="ExternalOutput")
        h_dbcf = nc.dram_tensor("dbg_bcf", [128, TB], F32,
                                kind="ExternalOutput")

    t_id = nc.inline_tensor(np.eye(128, dtype=np.float32), "id_f32")

    gctx = contextlib.ExitStack()

    def sb(name, shape, dtype):
        return gctx.enter_context(nc.sbuf_tensor(name, shape, dtype))

    xn_fm = sb("xn_fm", [128, DC, T], BF16)
    bc_sb = sb("bc_sb", [128, T], BF16)
    dl_sb = sb("dl_sb", [DTR, T], BF16)
    cmb_fm = sb("cmb_fm", [8, T], BF16)
    a_fm = sb("a_fm", [128, DC, NST], F32)
    dtpw_bf = sb("dtpw_bf", [DTR, D], BF16)
    xpw_bf = sb("xpw_bf", [128, DC, 192], BF16)
    rw_bf = sb("rw_bf", [128, DC, E], BF16)
    id_f = sb("id_f", [128, 128], F32)
    id_b = sb("id_b", [128, 128], BF16)
    ones_b = sb("ones_b", [1, 128], BF16)
    sml = sb("sml", [128, 5, DC], F32)
    dv_c = sml[:, 0, :]
    nb_c = sml[:, 1, :]
    nw_c = sml[:, 2, :]
    rsc_c = sml[:, 3, :]
    sts = sb("sts", [128, 48], F32)
    jnk = sb("jnk", [128, 64], F32)
    rr_all = sts[:, 32:48]
    pes_bc = sb("pes_bc", [128, 8], F32)
    cmb8 = sb("cmb8", [128, 8], BF16)

    ps = gctx.enter_context(nc.psum_tensor("ps_arena", [128, 4096], F32))

    def bank(b, cols=512, rows=128):
        return ps[0:rows, b * 512: b * 512 + cols]

    def fence(eng):
        if eng == "scalar":
            P.op(eng, lambda e: e.copy(jnk[:, 32:64], jnk[:, 0:32]))
            P.op(eng, lambda e: e.copy(jnk[:, 32:64], jnk[:, 0:32]))
        else:
            P.op(eng, lambda e: e.tensor_copy(jnk[:, 32:64], jnk[:, 0:32]))
            P.op(eng, lambda e: e.tensor_copy(jnk[:, 32:64], jnk[:, 0:32]))

    P.dma("sync", lambda e: e.dma_start(a_fm[:, :, :], h_alog.ap()))
    P.dma("sync", lambda e: e.dma_start(sml[:, :, :], h_sml.ap()))
    P.dma("sync", lambda e: e.dma_start(id_f[:, :], t_id.ap()))

    sctx = contextlib.ExitStack()
    dtpw_f = sctx.enter_context(nc.sbuf_tensor("dtpw_f", [DTR, D], F32))
    xpw_f = sctx.enter_context(nc.sbuf_tensor("xpw_f", [128, DC, 192], F32))
    rw_f = sctx.enter_context(nc.sbuf_tensor("rw_f", [128, DC, E], F32))
    P.dma("sync", lambda e: e.dma_start(dtpw_f[:, :], h_dtpw.ap()))
    P.dma("sync", lambda e: e.dma_start(xpw_f[:, :, :], h_xpw.ap()))
    P.dma("sync", lambda e: e.dma_start(rw_f[:, :, :], h_rw.ap()))
    w_pes = P.dma("sync", lambda e: e.dma_start(pes_bc[:, :], h_pes.ap()))
    WS = [(SP, w_pes)]

    P.op("vector", lambda e: e.memset(jnk[:, :], 0.0))
    P.op("vector", lambda e: e.memset(sts[:, 12:13], EPS_LN), waits=WS)
    P.op("vector", lambda e: e.memset(sts[:, 13:14], EPS_RMS), waits=WS)
    nc.const_aps.register(F32, EPS_LN, sts[:, 12:13])
    nc.const_aps.register(F32, EPS_RMS, sts[:, 13:14])
    P.op("vector", lambda e: e.memset(sts[:, 14:15], -3.4657359027997265),
         waits=WS)
    nc.const_aps.register(F32, -3.4657359027997265, sts[:, 14:15])

    c_ae = P.op("scalar", lambda e: e.activation(
        a_fm[:, :, :], a_fm[:, :, :], AF.Exp), waits=WS)
    P.op("vector", lambda e: e.tensor_scalar_mul(
        a_fm[:, :, :], a_fm[:, :, :], -1.0), waits=[(ACT, c_ae)])
    P.op("vector", lambda e: e.tensor_scalar_mul(nb_c, nb_c, -1.0), waits=WS)
    P.op("vector", lambda e: e.memset(ones_b[:, :], 1.0), waits=WS)
    for dc in range(DC):
        P.op("vector", lambda e, dc=dc: e.tensor_scalar_mul(
            rw_f[:, dc, :], rw_f[:, dc, :], rsc_c[:, dc:dc + 1]), waits=WS)
    c_vs = P.cnt[DVE]
    P.op("scalar", lambda e: e.copy(dtpw_bf[:, :], dtpw_f[:, :]), waits=WS)
    P.op("scalar", lambda e: e.copy(xpw_bf[:, :, :], xpw_f[:, :, :]), waits=WS)
    P.op("scalar", lambda e: e.copy(id_b[:, :], id_f[:, :]), waits=WS)
    c_setup_act = P.op("scalar", lambda e: e.copy(
        rw_bf[:, :, :], rw_f[:, :, :]), waits=[(DVE, c_vs)])
    a_neg_dve = P.cnt[DVE]

    p1 = contextlib.ExitStack()
    xt_r = [p1.enter_context(nc.sbuf_tensor(f"xt{k}", [128, D], F32))
            for k in range(3)]
    xq_r = [p1.enter_context(nc.sbuf_tensor(f"xq{k}", [128, D], F32))
            for k in range(2)]
    xnt_r = [p1.enter_context(nc.sbuf_tensor(f"xnt{k}", [128, D], BF16))
             for k in range(2)]
    xnm_r = [p1.enter_context(nc.sbuf_tensor(f"xnm{k}", [128, DC, 128], BF16))
             for k in range(2)]
    dgr_r = [p1.enter_context(nc.sbuf_tensor(f"dgr{k}", [128, 128], BF16))
             for k in range(2)]
    xf_r = [p1.enter_context(nc.sbuf_tensor(f"xf{k}", [128, DC, 128], F32))
            for k in range(2)]
    r8 = [p1.enter_context(nc.sbuf_tensor(f"r8{k}", [128, 8], F32))
          for k in range(6)]

    g_xt = Guard(3)
    g_xq = Guard(2)
    g_xnt = Guard(2)
    g_xnm = Guard(2)
    g_dgr = Guard(2)
    g_xf = Guard(2)
    g_tb01 = Guard(2)
    g_tb23 = Guard(2)
    g_b4 = Guard(1)
    g_b5 = Guard(1)
    g_b6 = Guard(1)
    g_b7 = Guard(1)

    mu, var_, istd, t1c, t2c, msc, m1c, nm1c, m2c, smc, rsv, rr32c = (
        sts[:, k:k + 1] for k in range(12))

    tbu = [0]
    for i in range(NTT):
        sl = slice(i * 128, (i + 1) * 128)
        xt, xq = xt_r[i % 3], xq_r[i % 2]
        xnt, xnm = xnt_r[i % 2], xnm_r[i % 2]
        dgr, xf = dgr_r[i % 2], xf_r[i % 2]

        w_x = P.dma("sync", lambda e, xt=xt, sl=sl: e.dma_start(
            xt[:, :], h_x.ap()[sl, :]), waits=g_xt.waits(i))
        WL = [(SP, w_x)]

        c_r1 = P.op("vector", lambda e, xt=xt: e.reduce_sum(
            t1c, xt[:, :], AX.X), waits=WL)
        c_sq = P.op("scalar", lambda e, xt=xt, xq=xq: e.activation(
            xq[:, :], xt[:, :], AF.Square), waits=WL + g_xq.waits(i))
        c_r2 = P.op("vector", lambda e, xq=xq: e.reduce_sum(
            t2c, xq[:, :], AX.X), waits=[(ACT, c_sq)])
        fence("vector")
        c_mu = P.op("vector", lambda e: e.tensor_scalar_mul(mu, t1c, 1.0 / D))
        fence("vector")
        P.op("vector", lambda e: e.tensor_scalar(t1c, mu, mu, -1.0,
                                                 ALU.mult, ALU.mult))
        P.op("vector", lambda e: e.tensor_scalar_mul(t2c, t2c, 1.0 / D))
        fence("vector")
        c_var = P.op("vector", lambda e: e.tensor_add(var_, t1c, t2c))
        P.op("scalar", lambda e: e.activation(var_, var_, AF.Ln, bias=EPS_LN),
             waits=[(DVE, c_var)])
        fence("scalar")
        c_istd = P.op("scalar", lambda e: e.activation(
            istd, var_, AF.Exp, scale=-0.5))
        c_xn = P.op("vector", lambda e, xnt=xnt, xt=xt: e.tensor_scalar(
            xnt[:, :], xt[:, :], mu, istd, ALU.subtract, ALU.mult),
            waits=[(ACT, c_istd)] + g_xnt.waits(i))
        c_sq2 = P.op("scalar", lambda e, xq=xq, xnt=xnt: e.activation(
            xq[:, :], xnt[:, :], AF.Square), waits=[(DVE, c_xn)])
        c_r3 = P.op("vector", lambda e, xq=xq: e.reduce_sum(
            msc, xq[:, :], AX.X), waits=[(ACT, c_sq2)])
        fence("vector")
        c_ms = P.op("vector", lambda e: e.tensor_scalar_mul(msc, msc, 1.0 / D))
        g_xq.set(i, (DVE, c_ms))
        P.op("scalar", lambda e: e.activation(msc, msc, AF.Ln, bias=EPS_RMS),
             waits=[(DVE, c_ms)])
        fence("scalar")
        rr = rr_all[:, i:i + 1]
        c_rr = P.op("scalar", lambda e, rr=rr: e.activation(
            rr, msc, AF.Exp, scale=-0.5))
        fence("scalar")
        c_rr32 = P.op("scalar", lambda e: e.activation(
            rr32c, msc, AF.Exp, scale=-0.5, bias=-3.4657359027997265))
        c_dg = P.op("vector", lambda e, dgr=dgr, rr=rr: e.tensor_scalar_mul(
            dgr[:, :], id_b[:, :], rr),
            waits=[(ACT, c_rr), (ACT, c_setup_act)] + g_dgr.waits(i))

        ev_last = 0
        for dc in range(DC):
            bidx = tbu[0]
            tbu[0] += 1
            c_mm = P.op("tensor", lambda e, xnt=xnt, dc=dc, b=bidx % 2:
                        e.matmul(bank(b, 128),
                                 xnt[:, dc * 128:(dc + 1) * 128], id_b[:, :],
                                 start=True, stop=True),
                        waits=[(DVE, c_xn), (ACT, c_setup_act)]
                        + g_tb01.waits(bidx))
            c_ev = P.op("scalar", lambda e, dc=dc, sl=sl, b=bidx % 2: e.copy(
                xn_fm[:, dc, sl], bank(b, 128)), waits=[(PE, c_mm)])
            g_tb01.set(bidx, (ACT, c_ev))
            ev_last = c_ev
        c_xn_ev = ev_last

        for dc in range(DC):
            bidx = tbu[0]
            tbu[0] += 1
            c_mm = P.op("tensor", lambda e, xnt=xnt, dgr=dgr, dc=dc,
                        b=bidx % 2: e.matmul(
                            bank(b, 128), xnt[:, dc * 128:(dc + 1) * 128],
                            dgr[:, :], start=True, stop=True),
                        waits=[(DVE, c_dg)] + g_tb01.waits(bidx))
            c_ev = P.op("scalar", lambda e, xnm=xnm, dc=dc, b=bidx % 2:
                        e.activation(xnm[:, dc, :], bank(b, 128), AF.Copy,
                                     scale=nw_c[:, dc:dc + 1]),
                        waits=[(PE, c_mm)] + g_xnm.waits(i))
            g_tb01.set(bidx, (ACT, c_ev))
        c_xnm_ev = P.cnt[ACT]
        g_xnt.set(i, (PE, P.cnt[PE]))
        g_dgr.set(i, (PE, P.cnt[PE]))

        for dc in range(DC):
            bidx = i * DC + dc
            c_mm = P.op("tensor", lambda e, xt=xt, dc=dc, b=bidx % 2:
                        e.matmul(bank(2 + b, 128),
                                 xt[:, dc * 128:(dc + 1) * 128], id_f[:, :],
                                 start=True, stop=True),
                        waits=WL + g_tb23.waits(bidx))
            c_cp = P.op("vector", lambda e, xf=xf, dc=dc, b=bidx % 2:
                        e.tensor_copy(xf[:, dc, :], bank(2 + b, 128)),
                        waits=[(PE, c_mm)] + (g_xf.waits(i) if dc == 0 else []))
            g_tb23.set(bidx, (DVE, c_cp))
        g_xt.set(i, (PE, P.cnt[PE]))
        c_xf = P.cnt[DVE]
        w_xf = P.dma("sync", lambda e, xf=xf, sl=sl: e.dma_start(
            bass.AP(h_xfm, sl.start, [[T, 128], [128 * T, DC], [1, 128]]),
            xf[:, :, :]), waits=[(DVE, c_xf)])
        g_xf.set(i, (SP, w_xf))

        for dc in range(DC):
            P.op("tensor", lambda e, xnm=xnm, dc=dc: e.matmul(
                bank(4, 128), xpw_bf[:, dc, 0:128], xnm[:, dc, :],
                start=(dc == 0), stop=(dc == DC - 1)),
                waits=([(ACT, c_xnm_ev)] + g_b4.waits(0)) if dc == 0 else [],
                inc=(1 if dc == DC - 1 else 0))
        c_p1m = P.cnt[PE]
        for dc in range(DC):
            P.op("tensor", lambda e, xnm=xnm, dc=dc: e.matmul(
                bank(5, 128, 64), xpw_bf[:, dc, 128:192], xnm[:, dc, :],
                start=(dc == 0), stop=(dc == DC - 1)),
                waits=g_b5.waits(0) if dc == 0 else [],
                inc=(1 if dc == DC - 1 else 0))
        c_p2m = P.cnt[PE]
        g_xnm.set(i, (PE, c_p2m))
        e1 = P.op("scalar", lambda e, sl=sl: e.copy(
            dl_sb[:, sl], ps[0:64, 4 * 512:4 * 512 + 128]),
            waits=[(PE, c_p1m)])
        e2 = P.op("scalar", lambda e, sl=sl: e.copy(
            bc_sb[64:128, sl], ps[64:128, 4 * 512:4 * 512 + 128]))
        g_b4.set(0, (ACT, e2))
        e3 = P.op("scalar", lambda e, sl=sl: e.copy(
            bc_sb[0:64, sl], ps[0:64, 5 * 512:5 * 512 + 128]),
            waits=[(PE, c_p2m)])
        g_b5.set(0, (ACT, e3))

        for dc in range(DC):
            P.op("tensor", lambda e, dc=dc, sl=sl: e.matmul(
                bank(6, 8), xn_fm[:, dc, sl], rw_bf[:, dc, :],
                start=(dc == 0), stop=(dc == DC - 1)),
                waits=([(ACT, c_xn_ev)] + g_b6.waits(0)) if dc == 0 else [],
                inc=(1 if dc == DC - 1 else 0))
        c_lg = P.cnt[PE]
        lgs, p8v, ge1, l2v, tmv, pmv = r8
        c0 = P.op("scalar", lambda e: e.activation(
            lgs[:, :], bank(6, 8), AF.Copy, scale=rr32c),
            waits=[(PE, c_lg), (ACT, c_rr32)])
        g_b6.set(0, (ACT, c0))
        c_m1 = P.op("vector", lambda e: e.reduce_max(m1c, lgs[:, :], AX.X),
                    waits=[(ACT, c0)])
        fence("vector")
        c_nm = P.op("vector", lambda e: e.tensor_scalar_mul(nm1c, m1c, -1.0))
        c_pa = P.op("scalar", lambda e: e.activation(
            p8v[:, :], lgs[:, :], AF.Exp, bias=nm1c), waits=[(DVE, c_nm)])
        c_g1 = P.op("vector", lambda e: e.tensor_single_scalar(
            ge1[:, :], lgs[:, :], m1c, ALU.is_ge))
        fence("vector")
        c_l2 = P.op("vector", lambda e: e.scalar_tensor_tensor(
            l2v[:, :], ge1[:, :], -BIGPOS, lgs[:, :], ALU.mult, ALU.add))
        fence("vector")
        c_m2 = P.op("vector", lambda e: e.reduce_max(m2c, l2v[:, :], AX.X))
        fence("vector")
        c_tm2 = P.op("vector", lambda e: e.tensor_single_scalar(
            tmv[:, :], lgs[:, :], m2c, ALU.is_ge))
        fence("vector")
        c_pm = P.op("vector", lambda e: e.tensor_mul(
            pmv[:, :], p8v[:, :], tmv[:, :]), waits=[(ACT, c_pa)])
        fence("vector")
        c_sm = P.op("vector", lambda e: e.reduce_sum(smc, pmv[:, :], AX.X))
        fence("vector")
        P.op("vector", lambda e: e.reciprocal(rsv, smc))
        fence("vector")
        P.op("vector", lambda e: e.tensor_scalar_mul(pmv[:, :], pmv[:, :], rsv))
        fence("vector")
        c_cm = P.op("vector", lambda e: e.tensor_mul(
            pmv[:, :], pmv[:, :], pes_bc[:, :]))
        c_cc = P.op("scalar", lambda e: e.copy(cmb8[:, :], pmv[:, :]),
                    waits=[(DVE, c_cm)] + g_b7.waits(0))
        c_tm = P.op("tensor", lambda e: e.matmul(
            ps[0:8, 7 * 512:7 * 512 + 128], cmb8[:, :], id_b[:, :],
            start=True, stop=True), waits=[(ACT, c_cc)] + g_b7.waits(0))
        c_ce = P.op("scalar", lambda e, sl=sl: e.copy(
            cmb_fm[0:8, sl], ps[0:8, 7 * 512:7 * 512 + 128]),
            waits=[(PE, c_tm)])
        g_b7.set(0, (ACT, c_ce), (PE, c_tm))

    p1_done = P.state()
    if debug:
        P.dma("sync", lambda e: e.dma_start(h_dxn.ap(), xn_fm[:, :, :]),
              waits=[(ACT, p1_done[ACT])])
        P.dma("sync", lambda e: e.dma_start(h_dbc.ap(), bc_sb[:, :]))
        P.dma("sync", lambda e: e.dma_start(h_ddl.ap(), dl_sb[:, :]))
        P.dma("sync", lambda e: e.dma_start(h_dcmb.ap(), cmb_fm[:, :]))
    p1.close()
    sctx.close()

    mx = contextlib.ExitStack()

    def mb(name, shape, dtype):
        return mx.enter_context(nc.sbuf_tensor(name, shape, dtype))

    dtm = mb("dtm", [128, DC, TB], BF16)
    u_bf = mb("u_bf", [128, DC, TB], BF16)
    y_acc = mb("y_acc", [128, DC, TB], F32)
    sg_r = [mb(f"sg{k}", [128, TB], F32) for k in range(2)]
    da_r = [mb(f"da{k}", [128, TB], F32) for k in range(2)]
    s_r = [mb(f"s{k}", [128, TB], BF16) for k in range(2)]
    hh_r = [mb(f"hm{k}", [128, TB], BF16) for k in range(2)]
    z_r = [mb(f"z{k}", [128, TB], BF16) for k in range(2)]
    bcB_r = [mb("bcB0", [128, TB], BF16)]
    bcC_r = [mb("bcC0", [128, TB], BF16)]
    stage = mb("stage", [1, 1, 2 * TB], BF16)
    rr_bc = mb("rr_bc", [128, T], BF16)
    rrT = mb("rrT", [16, 128], BF16)
    ybf = mb("ybf", [128, DC, TB], BF16)
    opw_bf = mb("opw_bf", [128, DC, D], BF16)
    opwf_r = [mb("opwf0", [128, D], F32)]
    xfr_r = [mb(f"xfr{k}", [128, FT], F32) for k in range(2)]
    evr_r = [mb(f"evr{k}", [128, FT], F32) for k in range(2)]

    def bcb(which, half):
        b = 2 + 2 * which + half
        return ps[:, b * 512:(b + 1) * 512]

    g_dt = Guard(2)
    g_bcB = Guard(1)
    g_bcC = Guard(1)
    g_sbB = Guard(1)
    g_sbC = Guard(1)
    g_stage = Guard(1)
    g_da = Guard(2)
    g_s = Guard(2)
    g_hm = Guard(2)
    g_z = Guard(2)
    g_sg = Guard(2)
    g_op67 = Guard(2)
    g_xfr = Guard(2)
    g_evr = Guard(2)
    g_opwf = Guard(1)

    opw_c = 0
    for dci in range(DC):
        w = P.dma("sync", lambda e, dci=dci: e.dma_start(
            opwf_r[0][:, :], h_opw.ap()[dci * 128:(dci + 1) * 128, :]),
            waits=g_opwf.waits(dci))
        opw_c = P.op("scalar", lambda e, dci=dci: e.copy(
            opw_bf[:, dci, :], opwf_r[0][:, :]), waits=[(SP, w)])
        g_opwf.set(dci, (ACT, opw_c))

    c_rc = P.op("scalar", lambda e: e.copy(jnk[:, 0:16].bitcast(BF16)[:, 0:16],
                                           rr_all),
                waits=[(ACT, p1_done[ACT]), (DVE, p1_done[DVE])])
    c_rt = P.op("tensor", lambda e: e.matmul(
        bank(0, 128, 16), jnk[:, 0:16].bitcast(BF16)[:, 0:16], id_b[:, :],
        start=True, stop=True), waits=[(ACT, c_rc)])
    c_re = P.op("scalar", lambda e: e.copy(rrT[0:16, :], bank(0, 128, 16)),
                waits=[(PE, c_rt)])
    rseq = stage[0:1, 0, :]
    w_rf = P.dma("sync", lambda e: e.dma_start(rseq, rrT[0:16, :]),
                 waits=[(ACT, c_re)])
    for q4 in range(4):
        c_rb = P.op("tensor", lambda e, q4=q4: e.matmul(
            bank(1, 512), ones_b[0:1, :],
            stage[0:1, 0, q4 * 512:(q4 + 1) * 512], start=True, stop=True),
            waits=[(SP, w_rf)] + ([(ACT, P.cnt["ACT"])] if q4 else []))
        P.op("scalar", lambda e, q4=q4: e.copy(
            rr_bc[:, q4 * 512:(q4 + 1) * 512], bank(1, 512)),
            waits=[(PE, c_rb)])
    c_rrbc = P.cnt["ACT"]

    g_stage.set(0, (PE, P.cnt[PE]))
    g_dt.set(0, (ACT, c_rrbc))
    g_dt.set(1, (ACT, c_rrbc))
    xnm_t = s_r[0]
    uidx = 0
    for tb in range(NTB):
        t0 = tb * TB
        tsl = slice(t0, t0 + TB)
        snap = P.state()

        for dc in range(DC):
            sg = sg_r[dc % 2]
            for half in range(2):
                hsl = slice(t0 + half * 512, t0 + (half + 1) * 512)
                c_mm = P.op("tensor", lambda e, dc=dc, hsl=hsl, half=half:
                            e.matmul(bank(half, 512),
                                     dtpw_bf[:, dc * 128:(dc + 1) * 128],
                                     dl_sb[:, hsl], start=True, stop=True),
                            waits=[(ACT, p1_done[ACT])] + g_dt.waits(half))
                c_sg = P.op("scalar", lambda e, sg=sg, dc=dc, half=half:
                            e.activation(sg[:, half * 512:(half + 1) * 512],
                                         bank(half, 512), AF.Sigmoid,
                                         scale=-1.0, bias=nb_c[:, dc:dc + 1]),
                            waits=[(PE, c_mm)] + (g_sg.waits(dc)
                                                  if half == 0 else []))
                g_dt.set(half, (ACT, c_sg))
            c_ln = P.op("scalar", lambda e, sg=sg: e.activation(
                sg[:, :], sg[:, :], AF.Ln))
            c_d1 = P.op("vector", lambda e, sg=sg, dc=dc: e.tensor_scalar_mul(
                bass.AP(dtm, dc * TB + 1,
                        [[DC * TB, 128], [CH, TB // CH], [1, CH - 1]]),
                bass.AP(sg, 1, [[TB, 128], [CH, TB // CH], [1, CH - 1]]),
                -1.0),
                waits=[(ACT, c_ln), (ACT, snap[ACT]), (GP, snap[GP])])
            c_d2 = P.op("vector", lambda e, dc=dc: e.memset(
                bass.AP(dtm, dc * TB, [[DC * TB, 128], [CH, TB // CH], [1, 1]]),
                BIGPOS))
            c_dx = P.op("vector", lambda e, dc=dc, tsl=tsl:
                        e.scalar_tensor_tensor(
                            xnm_t[:, :], xn_fm[:, dc, tsl], nw_c[:, dc:dc + 1],
                            rr_bc[:, tsl], ALU.mult, ALU.mult),
                        waits=[(ACT, c_rrbc)])
            c_d3 = P.op("vector", lambda e, sg=sg, dc=dc:
                        e.scalar_tensor_tensor(
                            u_bf[:, dc, :], sg[:, :], -1.0, xnm_t[:, :],
                            ALU.mult, ALU.mult))
            g_sg.set(dc, (DVE, c_d3))
        c_p2d = P.cnt[DVE]

        c_y0 = P.op("vector", lambda e: e.memset(y_acc[:, :, :], 0.0),
                    waits=[(GP, P.cnt[GP])])

        for n in range(NST):
            st = stage[0:1, 0, :]
            w_st = P.dma("sync", lambda e, st=st, n=n, t0=t0: e.dma_start(
                st, bass.AP(bc_sb, n * T + t0, [[64 * T, 2], [1, TB]])),
                waits=g_stage.waits(n) + [(ACT, p1_done[ACT])])
            for half in range(2):
                P.op("tensor", lambda e, half=half: e.matmul(
                    bcb(0, half), ones_b[0:1, :],
                    stage[0:1, 0, TB + half * 512:TB + (half + 1) * 512],
                    start=True, stop=True),
                    waits=([(SP, w_st)] + g_bcB.waits(0)) if half == 0 else [],
                    inc=(1 if half == 1 else 0))
            c_mB = P.cnt[PE]
            for half in range(2):
                P.op("tensor", lambda e, half=half: e.matmul(
                    bcb(1, half), ones_b[0:1, :],
                    stage[0:1, 0, half * 512:(half + 1) * 512],
                    start=True, stop=True),
                    waits=g_bcC.waits(0) if half == 0 else [],
                    inc=(1 if half == 1 else 0))
            c_mC = P.cnt[PE]
            g_stage.set(n, (PE, c_mC))
            bB = bcB_r[0]
            bC = bcC_r[0]
            c_eB = P.op("scalar", lambda e, bB=bB: e.copy(
                bB[:, :], ps[:, 2 * 512:4 * 512]),
                waits=[(PE, c_mB)] + g_sbB.waits(n))
            g_bcB.set(0, (ACT, c_eB))
            c_eC = P.op("scalar", lambda e, bC=bC: e.copy(
                bC[:, :], ps[:, 4 * 512:6 * 512]),
                waits=[(PE, c_mC)] + g_sbC.waits(n))
            g_bcC.set(0, (ACT, c_eC))

            sB_cons = []
            sC_cons = []
            for dc in range(DC):
                da = da_r[uidx % 2]
                sb_ = s_r[uidx % 2]
                hb = hh_r[uidx % 2]
                zb = z_r[uidx % 2]
                c_exp = P.op("scalar", lambda e, da=da, dc=dc, n=n:
                             e.activation(da[:, :], dtm[:, dc, :], AF.Exp,
                                          scale=a_fm[:, dc, n:n + 1]),
                             waits=[(DVE, c_p2d)] + g_da.waits(uidx))
                c_s = P.op("vector", lambda e, sb_=sb_, dc=dc, bB=bB:
                           e.tensor_mul(sb_[:, :], u_bf[:, dc, :], bB[:, :]),
                           waits=[(ACT, c_eB), (DVE, c_p2d)] + g_s.waits(uidx))
                sB_cons.append((DVE, c_s))
                c_scan = P.op("vector", lambda e, hb=hb, da=da, sb_=sb_:
                              e.tensor_tensor_scan(hb[:, :], da[:, :],
                                                   sb_[:, :], 0.0,
                                                   ALU.mult, ALU.add),
                              waits=[(ACT, c_exp)] + g_hm.waits(uidx))
                g_da.set(uidx, (DVE, c_scan))
                g_s.set(uidx, (DVE, c_scan))
                c_z = P.op("vector", lambda e, zb=zb, hb=hb, bC=bC:
                           e.tensor_mul(zb[:, :], hb[:, :], bC[:, :]),
                           waits=[(ACT, c_eC)] + g_z.waits(uidx))
                zdep = (DVE, c_z)
                sC_cons.append(zdep)
                g_hm.set(uidx, zdep)
                c_y = P.op("vector", lambda e, dc=dc, zb=zb: e.tensor_add(
                    y_acc[:, dc, :], y_acc[:, dc, :], zb[:, :]),
                    waits=[(DVE, c_y0)])
                g_z.set(uidx, (DVE, c_y))
                uidx += 1
            g_sbB.set(n, *sB_cons)
            g_sbC.set(n, *sC_cons)

        if debug:
            P.dma("sync", lambda e, tb=tb: e.dma_start(
                h_ddtm.ap()[tb], dtm[:, :, :]), waits=[(DVE, c_p2d)])
            P.dma("sync", lambda e, tb=tb: e.dma_start(
                h_du.ap()[tb], u_bf[:, :, :]))
            P.dma("sync", lambda e, tb=tb: e.dma_start(
                h_dy.ap()[tb], y_acc[:, :, :]), waits=[(GP, P.cnt[GP])])

        c_gpd = P.cnt[GP]
        for dc in range(DC):
            P.op("vector", lambda e, dc=dc, tsl=tsl: e.scalar_tensor_tensor(
                ybf[:, dc, :], xn_fm[:, dc, tsl], dv_c[:, dc:dc + 1],
                y_acc[:, dc, :], ALU.mult, ALU.add),
                waits=[(GP, c_gpd), (PE, snap[PE])])
        c_ybf = P.cnt[DVE]
        if debug:
            P.dma("sync", lambda e, tb=tb: e.dma_start(
                h_dybf.ap()[tb], ybf[:, :, :]), waits=[(DVE, c_ybf)])
        for q in range(2):
            qsl = slice(t0 + q * 512, t0 + (q + 1) * 512)
            for dco in range(DC):
                it = q * DC + dco
                par = it % 2
                w_xf = P.dma("sync", lambda e, dco=dco, qsl=qsl, par=par:
                             e.dma_start(xfr_r[par][:, :],
                                         h_xfm.ap()[dco, :, qsl]),
                             waits=[(SP, p1_done[SP])] + g_xfr.waits(it))
                for dci in range(DC):
                    P.op("tensor", lambda e, dci=dci, dco=dco, q=q, par=par:
                         e.matmul(bank(6 + par, 512),
                                  opw_bf[:, dci, dco * 128:(dco + 1) * 128],
                                  ybf[:, dci, q * 512:(q + 1) * 512],
                                  start=(dci == 0), stop=(dci == DC - 1)),
                         waits=([(DVE, c_ybf), (ACT, opw_c)]
                                + g_op67.waits(it)) if dci == 0 else [],
                         inc=(1 if dci == DC - 1 else 0))
                c_op = P.cnt[PE]
                c_ev = P.op("vector", lambda e, par=par: e.tensor_add(
                    evr_r[par][:, :], bank(6 + par, 512), xfr_r[par][:, :]),
                    waits=[(PE, c_op), (SP, w_xf)] + g_evr.waits(it))
                g_op67.set(it, (DVE, c_ev))
                g_xfr.set(it, (DVE, c_ev))
                w_at = P.dma("sync", lambda e, dco=dco, qsl=qsl, par=par:
                             e.dma_start(h_attn.ap()[dco, :, qsl],
                                         evr_r[par][:, :]),
                             waits=[(DVE, c_ev)])
                g_evr.set(it, (SP, w_at))

    mixer_done = P.state()
    mx.close()

    fx = contextlib.ExitStack()

    def fb(name, shape, dtype):
        return fx.enter_context(nc.sbuf_tensor(name, shape, dtype))

    acc = fb("acc", [128, DC, T], F32)
    ffx = contextlib.ExitStack()

    def fb2(name, shape, dtype):
        return ffx.enter_context(nc.sbuf_tensor(name, shape, dtype))

    wgf_r = [fb2(f"wgf{k}", [128, 2, DC, 128], F32) for k in range(2)]
    wgb_r = [fb2(f"wgb{k}", [128, 2, DC, 128], BF16) for k in range(2)]
    wlf_r = [fb2("wlf0", [128, D], F32)]
    wl_e = fb2("wl_e", [128, KCN, D], BF16)
    g0s_r = [fb2(f"g0s{k}", [128, FT], BF16) for k in range(2)]
    g1s_r = [fb2(f"g1s{k}", [128, FT], BF16) for k in range(2)]
    t1_r = [fb2(f"t1{k}", [128, FT], BF16) for k in range(2)]
    hup_r = [fb2(f"hup{k}", [128, FT], BF16) for k in range(2)]
    hin_r = [fb2(f"hin{k}", [128, FT], BF16) for k in range(2)]
    cmb_bc = fb2("cmb_bc", [128, T], BF16)
    cstage = fb2("cstage", [1, TB], BF16)
    jnk2 = fb2("jnk2", [128, FT], F32) if True else None

    g_wgf = Guard(2)
    g_wgb = Guard(2)
    g_wlf = Guard(1)
    g_wle = Guard(1)
    g_gb = Guard(8)
    g_g0s = Guard(2)
    g_g1s = Guard(2)
    g_t1 = Guard(2)
    g_hup = Guard(2)
    g_hin = Guard(2)
    g_cst = Guard(1)
    g_cbc = Guard(1)
    g_dn = Guard(1)

    acc_w = []
    for dc in range(DC):
        w = P.dma("sync", lambda e, dc=dc: e.dma_start(
            acc[:, dc, :], h_attn.ap()[dc, :, :]),
            waits=[(DVE, mixer_done[DVE]), (SP, mixer_done[SP])])
        acc_w.append(w)

    hidx = 0
    hw_at = {}
    for e_i in range(E):
        c_cbc = 0
        for hf in range(2):
            w_cs = P.dma("sync", lambda e, e_i=e_i, hf=hf: e.dma_start(
                cstage[0:1, :], cmb_fm[e_i:e_i + 1, hf * TB:(hf + 1) * TB]),
                waits=[(ACT, p1_done[ACT]), (ACT, c_cbc)] + g_cst.waits(0))
            c_pe_cs = 0
            for q4 in range(2):
                c_bcm = P.op("tensor", lambda e, q4=q4: e.matmul(
                    bank(0, 512), ones_b[0:1, :],
                    cstage[0:1, q4 * 512:(q4 + 1) * 512],
                    start=True, stop=True),
                    waits=[(SP, w_cs), (ACT, c_cbc)] + g_gb.waits(0))
                c_cbc = P.op("scalar", lambda e, hf=hf, q4=q4: e.copy(
                    cmb_bc[:, hf * TB + q4 * 512:hf * TB + (q4 + 1) * 512],
                    bank(0, 512)), waits=[(PE, c_bcm)] + g_cbc.waits(0))
                g_gb.set(0, (ACT, c_cbc))
                c_pe_cs = c_bcm
            g_cst.set(0, (PE, c_pe_cs))

        if debug and e_i == 0:
            w_dbg = P.dma("sync", lambda e: e.dma_start(
                h_dcbc.ap(), cmb_bc[:, :]),
                waits=[(ACT, P.cnt["ACT"])])
            g_cbc.set(0, (SP, w_dbg))
        up_dve_last = 0
        for kc in range(KCN):
            par = kc % 2
            w1 = P.dma("sync", lambda e, e_i=e_i, kc=kc, par=par: e.dma_start(
                wgf_r[par][:, 0, :, :],
                h_gw.ap()[e_i, 0, :, :, kc * 128:(kc + 1) * 128]),
                waits=g_wgf.waits(kc))
            w2 = P.dma("sync", lambda e, e_i=e_i, kc=kc, par=par: e.dma_start(
                wgf_r[par][:, 1, :, :],
                h_gw.ap()[e_i, 1, :, :, kc * 128:(kc + 1) * 128]))
            w3 = P.dma("sync", lambda e, e_i=e_i, kc=kc, par=par: e.dma_start(
                wlf_r[0][:, :], h_lw.ap()[e_i, kc * 128:(kc + 1) * 128, :]),
                waits=g_wlf.waits(kc))
            cg = P.op("gpsimd", lambda e, par=par: e.tensor_copy(
                wgb_r[par][:, :, :, :], wgf_r[par][:, :, :, :]),
                waits=[(SP, w2)] + g_wgb.waits(kc))
            g_wgf.set(kc, (GP, cg))
            cl = P.op("gpsimd", lambda e, kc=kc: e.tensor_copy(
                wl_e[:, kc, :], wlf_r[0][:, :]),
                waits=[(SP, w3)] + (g_wle.waits(0) if kc == 0 else []))
            if debug and e_i == 0 and kc == 0:
                P.dma("sync", lambda e: e.dma_start(
                    h_dwg.ap(), wgb_r[0][:, :, :, :]), waits=[(GP, cg)])
            g_wlf.set(kc, (GP, cl))

            bank_done = {}
            for g in range(2):
                for dck in range(DC):
                    for ti4 in range(NFT):
                        b = g * 4 + ti4
                        c = P.op("tensor", lambda e, g=g, dck=dck, par=par,
                                 ti4=ti4, b=b: e.matmul(
                                     bank(b, 512), wgb_r[par][:, g, dck, :],
                                     xn_fm[:, dck, ti4 * FT:(ti4 + 1) * FT],
                                     start=(dck == 0), stop=(dck == DC - 1)),
                                 waits=([(GP, cg)] + g_gb.waits(b))
                                 if (dck == 0 and ti4 == 0) else
                                 (g_gb.waits(b) if dck == 0 else []),
                                 inc=(1 if dck == DC - 1 else 0))
                        if dck == DC - 1:
                            bank_done[b] = c
            for ti in range(NFT):
                fsl = slice(ti * FT, (ti + 1) * FT)
                g0s = g0s_r[ti % 2]
                g1s = g1s_r[ti % 2]
                t1b = t1_r[ti % 2]
                hup = hup_r[hidx % 2]
                c_e0 = P.op("scalar", lambda e, g0s=g0s, b=ti:
                            e.activation(g0s[:, :], bank(b, 512),
                                         AF.Gelu_apprx_tanh),
                            waits=[(PE, bank_done[ti])] + g_g0s.waits(ti))
                g_gb.set(ti, (ACT, c_e0))
                c_e1 = P.op("scalar", lambda e, g1s=g1s, b=4 + ti:
                            e.activation(g1s[:, :], bank(b, 512), AF.Copy),
                            waits=[(PE, bank_done[4 + ti])] + g_g1s.waits(ti))
                g_gb.set(4 + ti, (ACT, c_e1))
                if debug and e_i == 0 and kc == 0 and ti == 0:
                    P.dma("sync", lambda e: e.dma_start(
                        h_dg0.ap(), g0s[:, :]), waits=[(ACT, c_e0)])
                    P.dma("sync", lambda e: e.dma_start(
                        h_dg1.ap(), g1s[:, :]), waits=[(ACT, c_e1)])
                c_t1 = P.op("vector", lambda e, t1b=t1b, g1s=g1s, fsl=fsl:
                            e.tensor_mul(t1b[:, :], g1s[:, :], cmb_bc[:, fsl]),
                            waits=[(ACT, c_e1), (ACT, c_cbc)] + g_t1.waits(ti))
                g_g1s.set(ti, (DVE, c_t1))
                c_h = P.op("vector", lambda e, hup=hup, g0s=g0s, t1b=t1b:
                           e.tensor_mul(hup[:, :], g0s[:, :], t1b[:, :]),
                           waits=[(ACT, c_e0)] + g_hup.waits(hidx))
                g_g0s.set(ti, (DVE, c_h),
                          (("SP", P.cnt["SP"]) if (debug and e_i == 0
                           and kc == 0 and ti == 0) else None))
                g_t1.set(ti, (DVE, c_h))
                w_h = P.dma("scalar", lambda e, e_i=e_i, kc=kc, fsl=fsl,
                            hup=hup: e.dma_start(
                                h_h.ap()[e_i % 2, kc, :, fsl], hup[:, :]),
                            waits=[(DVE, c_h)])
                hw_at[(kc, ti)] = w_h
                g_hup.set(hidx, (AD, w_h))
                hidx += 1
                up_dve_last = c_h
            g_wgb.set(kc, (PE, P.cnt[PE]))
        c_up_act = P.cnt[ACT]
        c_up_ad = P.cnt[AD]
        g_cbc.set(0, (DVE, up_dve_last))

        for ti in range(NFT):
            fsl = slice(ti * FT, (ti + 1) * FT)
            for kc in range(KCN):
                hin = hin_r[(ti * KCN + kc) % 2]
                w_hi = P.dma("scalar", lambda e, e_i=e_i, kc=kc, fsl=fsl,
                             hin=hin: e.dma_start(
                                 hin[:, :], h_h.ap()[e_i % 2, kc, :, fsl]),
                             waits=[(AD, hw_at[(kc, ti)])]
                             + g_hin.waits(ti * KCN + kc))
                for dcm in range(DC):
                    P.op("tensor", lambda e, kc=kc, dcm=dcm, hin=hin:
                         e.matmul(
                             ps[:, dcm * 512:(dcm + 1) * 512],
                             wl_e[:, kc, dcm * 128:(dcm + 1) * 128],
                             hin[:, :],
                             start=(kc == 0), stop=(kc == KCN - 1),
                             skip_group_check=True),
                         waits=([(AD, w_hi), (GP, P.cnt[GP]),
                                 (ACT, c_up_act)] + g_dn.waits(0))
                         if (dcm == 0 and kc == 0) else
                         ([(AD, w_hi)] if dcm == 0 else []),
                         inc=(1 if dcm == DC - 1 else 0))
                g_hin.set(ti * KCN + kc, (PE, P.cnt[PE]))
            c_dn = P.cnt[PE]
            evs = []
            for dcm in range(DC):
                c_ev = P.op("vector", lambda e, dcm=dcm, ti=ti:
                            e.tensor_tensor(
                                acc[:, dcm, ti * FT:(ti + 1) * FT],
                                ps[:, dcm * 512:(dcm + 1) * 512],
                                acc[:, dcm, ti * FT:(ti + 1) * FT],
                                ALU.add),
                            waits=[(PE, c_dn), (SP, acc_w[dcm])])
                evs.append((DVE, c_ev))
            g_dn.set(0, *evs)
            for b in range(8):
                g_gb.set(b, *evs)
        g_wle.set(0, (PE, P.cnt[PE]))

    ffn_done = P.state()
    ffx.close()

    outt_r = [fx.enter_context(
        nc.sbuf_tensor(f"outt{k}", [128, DC, 128], F32)) for k in range(2)]
    g_ot = Guard(2)
    g_tb = Guard(2)
    for i in range(NTT):
        sl = slice(i * 128, (i + 1) * 128)
        ot = outt_r[i % 2]
        evl = 0
        for dc in range(DC):
            bidx = i * DC + dc
            c_mm = P.op("tensor", lambda e, dc=dc, sl=sl, b=bidx % 2:
                        e.matmul(bank(b, 128), acc[:, dc, sl], id_f[:, :],
                                 start=True, stop=True),
                        waits=[(DVE, ffn_done[DVE])] + g_tb.waits(bidx))
            c_ev = P.op("scalar", lambda e, ot=ot, dc=dc, b=bidx % 2: e.copy(
                ot[:, dc, :], bank(b, 128)),
                waits=[(PE, c_mm)] + (g_ot.waits(i) if dc == 0 else []))
            g_tb.set(bidx, (ACT, c_ev))
            evl = c_ev
        w_o = P.dma("sync", lambda e, ot=ot, sl=sl: e.dma_start(
            bass.AP(h_out, sl.start * D, [[D, 128], [128, DC], [1, 128]]),
            ot[:, :, :]), waits=[(ACT, evl)])
        g_ot.set(i, (SP, w_o))

    ctx = P.emit(nc)
    nc._keepalive = (gctx, fx, ctx)
    return nc


def kernel(L0_attn_x, L0_attn_norm_w, L0_attn_x_proj_w, L0_attn_dt_proj_w,
           L0_attn_dt_proj_b, L0_attn_A_log, L0_attn_D, L0_attn_out_proj_w,
           L0_ffn_router_scale, L0_ffn_router_w, L0_ffn_gating_w,
           L0_ffn_linear_w, L0_ffn_per_expert_scale):
    f32 = np.float32
    x = np.ascontiguousarray(np.asarray(L0_attn_x, f32))
    gw = np.asarray(L0_ffn_gating_w, f32).transpose(0, 1, 3, 2)
    gw5 = np.ascontiguousarray(
        gw.reshape(E, 2, DC, 128, HID).transpose(0, 1, 3, 2, 4))
    lw = np.ascontiguousarray(np.asarray(L0_ffn_linear_w, f32))
    xpw3 = np.ascontiguousarray(
        np.asarray(L0_attn_x_proj_w, f32).T.reshape(DC, 128, 192)
        .transpose(1, 0, 2))
    dtpwT = np.ascontiguousarray(np.asarray(L0_attn_dt_proj_w, f32).T)
    opwT = np.ascontiguousarray(np.asarray(L0_attn_out_proj_w, f32).T)
    alog3 = np.ascontiguousarray(
        np.asarray(L0_attn_A_log, f32).reshape(DC, 128, NST).transpose(1, 0, 2))

    def col(v):
        return np.asarray(v, f32).reshape(DC, 128).T

    sml = np.ascontiguousarray(np.stack(
        [col(L0_attn_D), col(L0_attn_dt_proj_b), col(L0_attn_norm_w),
         col(L0_ffn_router_scale), np.zeros((128, DC), f32)], axis=1))
    rw3 = np.ascontiguousarray(
        np.asarray(L0_ffn_router_w, f32).reshape(DC, 128, E).transpose(1, 0, 2))
    pes2 = np.ascontiguousarray(np.broadcast_to(
        np.asarray(L0_ffn_per_expert_scale, f32).reshape(1, E), (128, E)))

    nc = build_nc(debug=DEBUG)
    in_maps = []
    for c in range(NCORES):
        in_maps.append(dict(
            x=np.ascontiguousarray(x[c]), gw5=gw5, lw=lw, xpw3=xpw3,
            dtpwT=dtpwT, opwT=opwT, alog3=alog3, sml_d=sml, rw3=rw3, pes2=pes2))
    res = run_bass_kernel_spmd(nc, in_maps, core_ids=list(range(NCORES)))
    out = np.stack([res.results[c]["out"] for c in range(NCORES)], axis=0)
    return out


# version 14 (final)
# speedup vs baseline: 1.1862x; 1.0022x over previous
import contextlib
import numpy as np
import concourse.bass as bass
import concourse.mybir as mybir
from concourse.bass_utils import run_bass_kernel_spmd

F32 = mybir.dt.float32
BF16 = mybir.dt.bfloat16
AF = mybir.ActivationFunctionType
ALU = mybir.AluOpType
AX = mybir.AxisListType

NCORES = 8
D = 1024
DC = 8
T = 2048
NTT = 16
NST = 64
CH = 64
TB = 1024
NTB = T // TB
HID = 2048
KCN = 16
E = 8
FT = 512
NFT = T // FT
DTR = 64
BIGPOS = 1.0e30
EPS_LN = 1e-5
EPS_RMS = 1e-6
Z_ON_GP_MOD = 1000000


class Prog:
    ENGINES = ("sync", "scalar", "vector", "tensor", "gpsimd")
    SEM_OF = {"sync": "SP", "scalar": "ACT", "vector": "DVE",
              "tensor": "PE", "gpsimd": "GP"}
    DMA_SEM_OF = {"sync": "SP", "scalar": "AD", "gpsimd": "GPD"}
    DMA_SEMS = ("SP", "AD", "GPD")

    def __init__(self):
        self.items = {e: [] for e in self.ENGINES}
        self.cnt = {s: 0 for s in ("SP", "ACT", "DVE", "PE", "GP", "AD", "GPD")}

    def op(self, eng, fn, waits=(), inc=1):
        sem = self.SEM_OF[eng]
        if inc:
            self.cnt[sem] += 1
        self.items[eng].append((fn, [w for w in waits if w and w[1] > 0],
                                sem if inc else None))
        return self.cnt[sem]

    def dma(self, eng, fn, waits=()):
        sem = self.DMA_SEM_OF[eng]
        self.cnt[sem] += 16
        self.items[eng].append((fn, [w for w in waits if w and w[1] > 0], sem))
        return self.cnt[sem]

    def state(self):
        return dict(self.cnt)

    def emit(self, nc):
        sems = {}
        ctx = contextlib.ExitStack()
        for name in self.cnt:
            sems[name] = ctx.enter_context(nc.semaphore("sem_" + name))
        final_waits = [(s, v) for s, v in self.cnt.items() if v > 0]

        def run(eng_name, e):
            own = self.SEM_OF[eng_name]
            last = {}
            for fn, waits, sem in self.items[eng_name]:
                for (ws, wv) in waits:
                    if ws == own and ws not in self.DMA_SEMS:
                        continue
                    if last.get(ws, -1) >= wv:
                        continue
                    e.wait_ge(sems[ws], wv)
                    last[ws] = wv
                inst = fn(e)
                if sem is not None:
                    inst.then_inc(sems[sem], 16 if sem in self.DMA_SEMS else 1)
            for ws, wv in final_waits:
                if ws == own and ws not in self.DMA_SEMS:
                    continue
                if last.get(ws, -1) >= wv:
                    continue
                e.wait_ge(sems[ws], wv)
                last[ws] = wv

        with nc.Block() as block:
            @block.sync
            def _(e):
                run("sync", e)

            @block.scalar
            def _(e):
                run("scalar", e)

            @block.vector
            def _(e):
                run("vector", e)

            @block.tensor
            def _(e):
                run("tensor", e)

            @block.gpsimd
            def _(e):
                run("gpsimd", e)
        return ctx


class Guard:

    def __init__(self, nslots):
        self.n = nslots
        self.v = [[] for _ in range(nslots)]

    def waits(self, i):
        return list(self.v[i % self.n])

    def set(self, i, *deps):
        self.v[i % self.n] = [d for d in deps if d and d[1] > 0]


DEBUG = False


def build_nc(debug=False):
    nc = bass.Bass("TRN2", target_bir_lowering=False,
                   detect_race_conditions=False)
    P = Prog()
    SP, AD, ACT, DVE, PE, GP = "SP", "AD", "ACT", "DVE", "PE", "GP"

    def dram_in(name, shape):
        return nc.dram_tensor(name, shape, F32, kind="ExternalInput")

    h_x = dram_in("x", [T, D])
    h_gw = dram_in("gw5", [E, 2, 128, DC, HID])
    h_lw = dram_in("lw", [E, HID, D])
    h_xpw = dram_in("xpw3", [128, DC, 192])
    h_dtpw = dram_in("dtpwT", [DTR, D])
    h_opw = dram_in("opwT", [D, D])
    h_alog = dram_in("alog3", [128, DC, NST])
    h_sml = dram_in("sml_d", [128, 5, DC])
    h_rw = dram_in("rw3", [128, DC, E])
    h_pes = dram_in("pes2", [128, E])

    h_out = nc.dram_tensor("out", [T, D], F32, kind="ExternalOutput")
    scratch_kind = "ExternalOutput" if debug else "Internal"
    h_attn = nc.dram_tensor("attn_scr", [DC, 128, T], F32, kind=scratch_kind)
    h_xfm = nc.dram_tensor("xfm_scr", [DC, 128, T], F32, kind=scratch_kind)
    h_h = nc.dram_tensor("h_scr", [2, KCN, 128, T], BF16, kind=scratch_kind)
    if debug:
        h_dxn = nc.dram_tensor("dbg_xn", [128, DC, T], BF16,
                               kind="ExternalOutput")
        h_dbc = nc.dram_tensor("dbg_bc", [128, T], BF16,
                               kind="ExternalOutput")
        h_ddl = nc.dram_tensor("dbg_dl", [DTR, T], BF16,
                               kind="ExternalOutput")
        h_dcmb = nc.dram_tensor("dbg_cmb", [8, T], BF16,
                                kind="ExternalOutput")
        h_ddtm = nc.dram_tensor("dbg_dtm", [NTB, 128, DC, TB], BF16,
                                kind="ExternalOutput")
        h_du = nc.dram_tensor("dbg_u", [NTB, 128, DC, TB], BF16,
                              kind="ExternalOutput")
        h_dy = nc.dram_tensor("dbg_y", [NTB, 128, DC, TB], F32,
                              kind="ExternalOutput")
        h_dybf = nc.dram_tensor("dbg_ybf", [NTB, 128, DC, TB], BF16,
                                kind="ExternalOutput")
        h_dcbc = nc.dram_tensor("dbg_cbc", [128, T], BF16,
                                kind="ExternalOutput")
        h_dcst = nc.dram_tensor("dbg_cst", [1, TB], F32,
                                kind="ExternalOutput")
        h_dg0 = nc.dram_tensor("dbg_g0", [128, FT], BF16,
                               kind="ExternalOutput")
        h_dwg = nc.dram_tensor("dbg_wg", [128, 2, DC, 128], BF16,
                               kind="ExternalOutput")
        h_draw = nc.dram_tensor("dbg_raw", [128, FT], F32,
                                kind="ExternalOutput")
        h_dg1 = nc.dram_tensor("dbg_g1", [128, FT], BF16,
                               kind="ExternalOutput")
        h_dbcf = nc.dram_tensor("dbg_bcf", [128, TB], F32,
                                kind="ExternalOutput")

    t_id = nc.inline_tensor(np.eye(128, dtype=np.float32), "id_f32")

    gctx = contextlib.ExitStack()

    def sb(name, shape, dtype):
        return gctx.enter_context(nc.sbuf_tensor(name, shape, dtype))

    xn_fm = sb("xn_fm", [128, DC, T], BF16)
    bc_sb = sb("bc_sb", [128, T], BF16)
    dl_sb = sb("dl_sb", [DTR, T], BF16)
    cmb_fm = sb("cmb_fm", [8, T], BF16)
    a_fm = sb("a_fm", [128, DC, NST], F32)
    dtpw_bf = sb("dtpw_bf", [DTR, D], BF16)
    xpw_bf = sb("xpw_bf", [128, DC, 192], BF16)
    rw_bf = sb("rw_bf", [128, DC, E], BF16)
    id_f = sb("id_f", [128, 128], F32)
    id_b = sb("id_b", [128, 128], BF16)
    ones_b = sb("ones_b", [1, 128], BF16)
    sml = sb("sml", [128, 5, DC], F32)
    dv_c = sml[:, 0, :]
    nb_c = sml[:, 1, :]
    nw_c = sml[:, 2, :]
    rsc_c = sml[:, 3, :]
    sts = sb("sts", [128, 48], F32)
    jnk = sb("jnk", [128, 64], F32)
    rr_all = sts[:, 32:48]
    pes_bc = sb("pes_bc", [128, 8], F32)
    cmb8 = sb("cmb8", [128, 8], BF16)

    ps = gctx.enter_context(nc.psum_tensor("ps_arena", [128, 4096], F32))

    def bank(b, cols=512, rows=128):
        return ps[0:rows, b * 512: b * 512 + cols]

    def fence(eng):
        if eng == "scalar":
            P.op(eng, lambda e: e.copy(jnk[:, 32:64], jnk[:, 0:32]))
            P.op(eng, lambda e: e.copy(jnk[:, 32:64], jnk[:, 0:32]))
        else:
            P.op(eng, lambda e: e.tensor_copy(jnk[:, 32:64], jnk[:, 0:32]))
            P.op(eng, lambda e: e.tensor_copy(jnk[:, 32:64], jnk[:, 0:32]))

    P.dma("sync", lambda e: e.dma_start(a_fm[:, :, :], h_alog.ap()))
    P.dma("sync", lambda e: e.dma_start(sml[:, :, :], h_sml.ap()))
    P.dma("sync", lambda e: e.dma_start(id_f[:, :], t_id.ap()))

    sctx = contextlib.ExitStack()
    dtpw_f = sctx.enter_context(nc.sbuf_tensor("dtpw_f", [DTR, D], F32))
    xpw_f = sctx.enter_context(nc.sbuf_tensor("xpw_f", [128, DC, 192], F32))
    rw_f = sctx.enter_context(nc.sbuf_tensor("rw_f", [128, DC, E], F32))
    P.dma("sync", lambda e: e.dma_start(dtpw_f[:, :], h_dtpw.ap()))
    P.dma("sync", lambda e: e.dma_start(xpw_f[:, :, :], h_xpw.ap()))
    P.dma("sync", lambda e: e.dma_start(rw_f[:, :, :], h_rw.ap()))
    w_pes = P.dma("sync", lambda e: e.dma_start(pes_bc[:, :], h_pes.ap()))
    WS = [(SP, w_pes)]

    P.op("vector", lambda e: e.memset(jnk[:, :], 0.0))
    P.op("vector", lambda e: e.memset(sts[:, 12:13], EPS_LN), waits=WS)
    P.op("vector", lambda e: e.memset(sts[:, 13:14], EPS_RMS), waits=WS)
    nc.const_aps.register(F32, EPS_LN, sts[:, 12:13])
    nc.const_aps.register(F32, EPS_RMS, sts[:, 13:14])
    P.op("vector", lambda e: e.memset(sts[:, 14:15], -3.4657359027997265),
         waits=WS)
    nc.const_aps.register(F32, -3.4657359027997265, sts[:, 14:15])

    c_ae = P.op("scalar", lambda e: e.activation(
        a_fm[:, :, :], a_fm[:, :, :], AF.Exp), waits=WS)
    P.op("vector", lambda e: e.tensor_scalar_mul(
        a_fm[:, :, :], a_fm[:, :, :], -1.0), waits=[(ACT, c_ae)])
    P.op("vector", lambda e: e.tensor_scalar_mul(nb_c, nb_c, -1.0), waits=WS)
    P.op("vector", lambda e: e.memset(ones_b[:, :], 1.0), waits=WS)
    for dc in range(DC):
        P.op("vector", lambda e, dc=dc: e.tensor_scalar_mul(
            rw_f[:, dc, :], rw_f[:, dc, :], rsc_c[:, dc:dc + 1]), waits=WS)
    c_vs = P.cnt[DVE]
    P.op("scalar", lambda e: e.copy(dtpw_bf[:, :], dtpw_f[:, :]), waits=WS)
    P.op("scalar", lambda e: e.copy(xpw_bf[:, :, :], xpw_f[:, :, :]), waits=WS)
    P.op("scalar", lambda e: e.copy(id_b[:, :], id_f[:, :]), waits=WS)
    c_setup_act = P.op("scalar", lambda e: e.copy(
        rw_bf[:, :, :], rw_f[:, :, :]), waits=[(DVE, c_vs)])
    a_neg_dve = P.cnt[DVE]

    p1 = contextlib.ExitStack()
    xt_r = [p1.enter_context(nc.sbuf_tensor(f"xt{k}", [128, D], F32))
            for k in range(3)]
    xq_r = [p1.enter_context(nc.sbuf_tensor(f"xq{k}", [128, D], F32))
            for k in range(2)]
    xnt_r = [p1.enter_context(nc.sbuf_tensor(f"xnt{k}", [128, D], BF16))
             for k in range(2)]
    xnm_r = [p1.enter_context(nc.sbuf_tensor(f"xnm{k}", [128, DC, 128], BF16))
             for k in range(2)]
    dgr_r = [p1.enter_context(nc.sbuf_tensor(f"dgr{k}", [128, 128], BF16))
             for k in range(2)]
    xf_r = [p1.enter_context(nc.sbuf_tensor(f"xf{k}", [128, DC, 128], F32))
            for k in range(2)]
    r8 = [p1.enter_context(nc.sbuf_tensor(f"r8{k}", [128, 8], F32))
          for k in range(6)]

    g_xt = Guard(3)
    g_xq = Guard(2)
    g_xnt = Guard(2)
    g_xnm = Guard(2)
    g_dgr = Guard(2)
    g_xf = Guard(2)
    g_tb01 = Guard(2)
    g_tb23 = Guard(2)
    g_b4 = Guard(1)
    g_b5 = Guard(1)
    g_b6 = Guard(1)
    g_b7 = Guard(1)

    mu, var_, istd, t1c, t2c, msc, m1c, nm1c, m2c, smc, rsv, rr32c = (
        sts[:, k:k + 1] for k in range(12))

    tbu = [0]
    for i in range(NTT):
        sl = slice(i * 128, (i + 1) * 128)
        xt, xq = xt_r[i % 3], xq_r[i % 2]
        xnt, xnm = xnt_r[i % 2], xnm_r[i % 2]
        dgr, xf = dgr_r[i % 2], xf_r[i % 2]

        w_x = P.dma("sync", lambda e, xt=xt, sl=sl: e.dma_start(
            xt[:, :], h_x.ap()[sl, :]), waits=g_xt.waits(i))
        WL = [(SP, w_x)]

        c_r1 = P.op("vector", lambda e, xt=xt: e.reduce_sum(
            t1c, xt[:, :], AX.X), waits=WL)
        c_sq = P.op("scalar", lambda e, xt=xt, xq=xq: e.activation(
            xq[:, :], xt[:, :], AF.Square), waits=WL + g_xq.waits(i))
        c_r2 = P.op("vector", lambda e, xq=xq: e.reduce_sum(
            t2c, xq[:, :], AX.X), waits=[(ACT, c_sq)])
        fence("vector")
        c_mu = P.op("vector", lambda e: e.tensor_scalar_mul(mu, t1c, 1.0 / D))
        fence("vector")
        P.op("vector", lambda e: e.tensor_scalar(t1c, mu, mu, -1.0,
                                                 ALU.mult, ALU.mult))
        P.op("vector", lambda e: e.tensor_scalar_mul(t2c, t2c, 1.0 / D))
        fence("vector")
        c_var = P.op("vector", lambda e: e.tensor_add(var_, t1c, t2c))
        P.op("scalar", lambda e: e.activation(var_, var_, AF.Ln, bias=EPS_LN),
             waits=[(DVE, c_var)])
        fence("scalar")
        c_istd = P.op("scalar", lambda e: e.activation(
            istd, var_, AF.Exp, scale=-0.5))
        c_xn = P.op("vector", lambda e, xnt=xnt, xt=xt: e.tensor_scalar(
            xnt[:, :], xt[:, :], mu, istd, ALU.subtract, ALU.mult),
            waits=[(ACT, c_istd)] + g_xnt.waits(i))
        c_sq2 = P.op("scalar", lambda e, xq=xq, xnt=xnt: e.activation(
            xq[:, :], xnt[:, :], AF.Square), waits=[(DVE, c_xn)])
        c_r3 = P.op("vector", lambda e, xq=xq: e.reduce_sum(
            msc, xq[:, :], AX.X), waits=[(ACT, c_sq2)])
        fence("vector")
        c_ms = P.op("vector", lambda e: e.tensor_scalar_mul(msc, msc, 1.0 / D))
        g_xq.set(i, (DVE, c_ms))
        P.op("scalar", lambda e: e.activation(msc, msc, AF.Ln, bias=EPS_RMS),
             waits=[(DVE, c_ms)])
        fence("scalar")
        rr = rr_all[:, i:i + 1]
        c_rr = P.op("scalar", lambda e, rr=rr: e.activation(
            rr, msc, AF.Exp, scale=-0.5))
        fence("scalar")
        c_rr32 = P.op("scalar", lambda e: e.activation(
            rr32c, msc, AF.Exp, scale=-0.5, bias=-3.4657359027997265))
        c_dg = P.op("vector", lambda e, dgr=dgr, rr=rr: e.tensor_scalar_mul(
            dgr[:, :], id_b[:, :], rr),
            waits=[(ACT, c_rr), (ACT, c_setup_act)] + g_dgr.waits(i))

        ev_last = 0
        for dc in range(DC):
            bidx = tbu[0]
            tbu[0] += 1
            c_mm = P.op("tensor", lambda e, xnt=xnt, dc=dc, b=bidx % 2:
                        e.matmul(bank(b, 128),
                                 xnt[:, dc * 128:(dc + 1) * 128], id_b[:, :],
                                 start=True, stop=True),
                        waits=[(DVE, c_xn), (ACT, c_setup_act)]
                        + g_tb01.waits(bidx))
            c_ev = P.op("scalar", lambda e, dc=dc, sl=sl, b=bidx % 2: e.copy(
                xn_fm[:, dc, sl], bank(b, 128)), waits=[(PE, c_mm)])
            g_tb01.set(bidx, (ACT, c_ev))
            ev_last = c_ev
        c_xn_ev = ev_last

        for dc in range(DC):
            bidx = tbu[0]
            tbu[0] += 1
            c_mm = P.op("tensor", lambda e, xnt=xnt, dgr=dgr, dc=dc,
                        b=bidx % 2: e.matmul(
                            bank(b, 128), xnt[:, dc * 128:(dc + 1) * 128],
                            dgr[:, :], start=True, stop=True),
                        waits=[(DVE, c_dg)] + g_tb01.waits(bidx))
            c_ev = P.op("scalar", lambda e, xnm=xnm, dc=dc, b=bidx % 2:
                        e.activation(xnm[:, dc, :], bank(b, 128), AF.Copy,
                                     scale=nw_c[:, dc:dc + 1]),
                        waits=[(PE, c_mm)] + g_xnm.waits(i))
            g_tb01.set(bidx, (ACT, c_ev))
        c_xnm_ev = P.cnt[ACT]
        g_xnt.set(i, (PE, P.cnt[PE]))
        g_dgr.set(i, (PE, P.cnt[PE]))

        for dc in range(DC):
            bidx = i * DC + dc
            c_mm = P.op("tensor", lambda e, xt=xt, dc=dc, b=bidx % 2:
                        e.matmul(bank(2 + b, 128),
                                 xt[:, dc * 128:(dc + 1) * 128], id_f[:, :],
                                 start=True, stop=True),
                        waits=WL + g_tb23.waits(bidx))
            c_cp = P.op("vector", lambda e, xf=xf, dc=dc, b=bidx % 2:
                        e.tensor_copy(xf[:, dc, :], bank(2 + b, 128)),
                        waits=[(PE, c_mm)] + (g_xf.waits(i) if dc == 0 else []))
            g_tb23.set(bidx, (DVE, c_cp))
        g_xt.set(i, (PE, P.cnt[PE]))
        c_xf = P.cnt[DVE]
        w_xf = P.dma("sync", lambda e, xf=xf, sl=sl: e.dma_start(
            bass.AP(h_xfm, sl.start, [[T, 128], [128 * T, DC], [1, 128]]),
            xf[:, :, :]), waits=[(DVE, c_xf)])
        g_xf.set(i, (SP, w_xf))

        for dc in range(DC):
            P.op("tensor", lambda e, xnm=xnm, dc=dc: e.matmul(
                bank(4, 128), xpw_bf[:, dc, 0:128], xnm[:, dc, :],
                start=(dc == 0), stop=(dc == DC - 1)),
                waits=([(ACT, c_xnm_ev)] + g_b4.waits(0)) if dc == 0 else [],
                inc=(1 if dc == DC - 1 else 0))
        c_p1m = P.cnt[PE]
        for dc in range(DC):
            P.op("tensor", lambda e, xnm=xnm, dc=dc: e.matmul(
                bank(5, 128, 64), xpw_bf[:, dc, 128:192], xnm[:, dc, :],
                start=(dc == 0), stop=(dc == DC - 1)),
                waits=g_b5.waits(0) if dc == 0 else [],
                inc=(1 if dc == DC - 1 else 0))
        c_p2m = P.cnt[PE]
        g_xnm.set(i, (PE, c_p2m))
        e1 = P.op("scalar", lambda e, sl=sl: e.copy(
            dl_sb[:, sl], ps[0:64, 4 * 512:4 * 512 + 128]),
            waits=[(PE, c_p1m)])
        e2 = P.op("scalar", lambda e, sl=sl: e.copy(
            bc_sb[64:128, sl], ps[64:128, 4 * 512:4 * 512 + 128]))
        g_b4.set(0, (ACT, e2))
        e3 = P.op("scalar", lambda e, sl=sl: e.copy(
            bc_sb[0:64, sl], ps[0:64, 5 * 512:5 * 512 + 128]),
            waits=[(PE, c_p2m)])
        g_b5.set(0, (ACT, e3))

        for dc in range(DC):
            P.op("tensor", lambda e, dc=dc, sl=sl: e.matmul(
                bank(6, 8), xn_fm[:, dc, sl], rw_bf[:, dc, :],
                start=(dc == 0), stop=(dc == DC - 1)),
                waits=([(ACT, c_xn_ev)] + g_b6.waits(0)) if dc == 0 else [],
                inc=(1 if dc == DC - 1 else 0))
        c_lg = P.cnt[PE]
        lgs, p8v, ge1, l2v, tmv, pmv = r8
        c0 = P.op("scalar", lambda e: e.activation(
            lgs[:, :], bank(6, 8), AF.Copy, scale=rr32c),
            waits=[(PE, c_lg), (ACT, c_rr32)])
        g_b6.set(0, (ACT, c0))
        c_m1 = P.op("vector", lambda e: e.reduce_max(m1c, lgs[:, :], AX.X),
                    waits=[(ACT, c0)])
        fence("vector")
        c_nm = P.op("vector", lambda e: e.tensor_scalar_mul(nm1c, m1c, -1.0))
        c_pa = P.op("scalar", lambda e: e.activation(
            p8v[:, :], lgs[:, :], AF.Exp, bias=nm1c), waits=[(DVE, c_nm)])
        c_g1 = P.op("vector", lambda e: e.tensor_single_scalar(
            ge1[:, :], lgs[:, :], m1c, ALU.is_ge))
        fence("vector")
        c_l2 = P.op("vector", lambda e: e.scalar_tensor_tensor(
            l2v[:, :], ge1[:, :], -BIGPOS, lgs[:, :], ALU.mult, ALU.add))
        fence("vector")
        c_m2 = P.op("vector", lambda e: e.reduce_max(m2c, l2v[:, :], AX.X))
        fence("vector")
        c_tm2 = P.op("vector", lambda e: e.tensor_single_scalar(
            tmv[:, :], lgs[:, :], m2c, ALU.is_ge))
        fence("vector")
        c_pm = P.op("vector", lambda e: e.tensor_mul(
            pmv[:, :], p8v[:, :], tmv[:, :]), waits=[(ACT, c_pa)])
        fence("vector")
        c_sm = P.op("vector", lambda e: e.reduce_sum(smc, pmv[:, :], AX.X))
        fence("vector")
        P.op("vector", lambda e: e.reciprocal(rsv, smc))
        fence("vector")
        P.op("vector", lambda e: e.tensor_scalar_mul(pmv[:, :], pmv[:, :], rsv))
        fence("vector")
        c_cm = P.op("vector", lambda e: e.tensor_mul(
            pmv[:, :], pmv[:, :], pes_bc[:, :]))
        c_cc = P.op("scalar", lambda e: e.copy(cmb8[:, :], pmv[:, :]),
                    waits=[(DVE, c_cm)] + g_b7.waits(0))
        c_tm = P.op("tensor", lambda e: e.matmul(
            ps[0:8, 7 * 512:7 * 512 + 128], cmb8[:, :], id_b[:, :],
            start=True, stop=True), waits=[(ACT, c_cc)] + g_b7.waits(0))
        c_ce = P.op("scalar", lambda e, sl=sl: e.copy(
            cmb_fm[0:8, sl], ps[0:8, 7 * 512:7 * 512 + 128]),
            waits=[(PE, c_tm)])
        g_b7.set(0, (ACT, c_ce), (PE, c_tm))

    p1_done = P.state()
    if debug:
        P.dma("sync", lambda e: e.dma_start(h_dxn.ap(), xn_fm[:, :, :]),
              waits=[(ACT, p1_done[ACT])])
        P.dma("sync", lambda e: e.dma_start(h_dbc.ap(), bc_sb[:, :]))
        P.dma("sync", lambda e: e.dma_start(h_ddl.ap(), dl_sb[:, :]))
        P.dma("sync", lambda e: e.dma_start(h_dcmb.ap(), cmb_fm[:, :]))
    p1.close()
    sctx.close()

    mx = contextlib.ExitStack()

    def mb(name, shape, dtype):
        return mx.enter_context(nc.sbuf_tensor(name, shape, dtype))

    dtm = mb("dtm", [128, DC, TB], BF16)
    u_bf = mb("u_bf", [128, DC, TB], BF16)
    y_acc = mb("y_acc", [128, DC, TB], F32)
    sg_r = [mb(f"sg{k}", [128, TB], F32) for k in range(2)]
    da_r = [mb(f"da{k}", [128, TB], F32) for k in range(2)]
    s_r = [mb(f"s{k}", [128, TB], BF16) for k in range(2)]
    hh_r = [mb(f"hm{k}", [128, TB], BF16) for k in range(2)]
    z_r = [mb(f"z{k}", [128, TB], BF16) for k in range(2)]
    bcB_r = [mb("bcB0", [128, TB], BF16)]
    bcC_r = [mb("bcC0", [128, TB], BF16)]
    stage = mb("stage", [1, 1, 2 * TB], BF16)
    rr_bc = mb("rr_bc", [128, T], BF16)
    rrT = mb("rrT", [16, 128], BF16)
    ybf = mb("ybf", [128, DC, TB], BF16)
    opw_bf = mb("opw_bf", [128, DC, D], BF16)
    opwf_r = [mb("opwf0", [128, D], F32)]
    xfr_r = [mb(f"xfr{k}", [128, FT], F32) for k in range(2)]
    evr_r = [mb(f"evr{k}", [128, FT], F32) for k in range(2)]

    def bcb(which, half):
        b = 2 + 2 * which + half
        return ps[:, b * 512:(b + 1) * 512]

    g_dt = Guard(2)
    g_bcB = Guard(1)
    g_bcC = Guard(1)
    g_sbB = Guard(1)
    g_sbC = Guard(1)
    g_stage = Guard(1)
    g_da = Guard(2)
    g_s = Guard(2)
    g_hm = Guard(2)
    g_z = Guard(2)
    g_sg = Guard(2)
    g_op67 = Guard(2)
    g_xfr = Guard(2)
    g_evr = Guard(2)
    g_opwf = Guard(1)

    opw_c = 0
    for dci in range(DC):
        w = P.dma("sync", lambda e, dci=dci: e.dma_start(
            opwf_r[0][:, :], h_opw.ap()[dci * 128:(dci + 1) * 128, :]),
            waits=g_opwf.waits(dci))
        opw_c = P.op("scalar", lambda e, dci=dci: e.copy(
            opw_bf[:, dci, :], opwf_r[0][:, :]), waits=[(SP, w)])
        g_opwf.set(dci, (ACT, opw_c))

    c_rc = P.op("scalar", lambda e: e.copy(jnk[:, 0:16].bitcast(BF16)[:, 0:16],
                                           rr_all),
                waits=[(ACT, p1_done[ACT]), (DVE, p1_done[DVE])])
    c_rt = P.op("tensor", lambda e: e.matmul(
        bank(0, 128, 16), jnk[:, 0:16].bitcast(BF16)[:, 0:16], id_b[:, :],
        start=True, stop=True), waits=[(ACT, c_rc)])
    c_re = P.op("scalar", lambda e: e.copy(rrT[0:16, :], bank(0, 128, 16)),
                waits=[(PE, c_rt)])
    rseq = stage[0:1, 0, :]
    w_rf = P.dma("sync", lambda e: e.dma_start(rseq, rrT[0:16, :]),
                 waits=[(ACT, c_re)])
    for q4 in range(4):
        c_rb = P.op("tensor", lambda e, q4=q4: e.matmul(
            bank(1, 512), ones_b[0:1, :],
            stage[0:1, 0, q4 * 512:(q4 + 1) * 512], start=True, stop=True),
            waits=[(SP, w_rf)] + ([(ACT, P.cnt["ACT"])] if q4 else []))
        P.op("scalar", lambda e, q4=q4: e.copy(
            rr_bc[:, q4 * 512:(q4 + 1) * 512], bank(1, 512)),
            waits=[(PE, c_rb)])
    c_rrbc = P.cnt["ACT"]

    g_stage.set(0, (PE, P.cnt[PE]))
    g_dt.set(0, (ACT, c_rrbc))
    g_dt.set(1, (ACT, c_rrbc))
    xnm_t = s_r[0]
    uidx = 0
    for tb in range(NTB):
        t0 = tb * TB
        tsl = slice(t0, t0 + TB)
        snap = P.state()

        for dc in range(DC):
            sg = sg_r[dc % 2]
            for half in range(2):
                hsl = slice(t0 + half * 512, t0 + (half + 1) * 512)
                c_mm = P.op("tensor", lambda e, dc=dc, hsl=hsl, half=half:
                            e.matmul(bank(half, 512),
                                     dtpw_bf[:, dc * 128:(dc + 1) * 128],
                                     dl_sb[:, hsl], start=True, stop=True),
                            waits=[(ACT, p1_done[ACT])] + g_dt.waits(half))
                c_sg = P.op("scalar", lambda e, sg=sg, dc=dc, half=half:
                            e.activation(sg[:, half * 512:(half + 1) * 512],
                                         bank(half, 512), AF.Sigmoid,
                                         scale=-1.0, bias=nb_c[:, dc:dc + 1]),
                            waits=[(PE, c_mm)] + (g_sg.waits(dc)
                                                  if half == 0 else []))
                g_dt.set(half, (ACT, c_sg))
            c_ln = P.op("scalar", lambda e, sg=sg: e.activation(
                sg[:, :], sg[:, :], AF.Ln))
            c_d1 = P.op("vector", lambda e, sg=sg, dc=dc: e.tensor_scalar_mul(
                bass.AP(dtm, dc * TB + 1,
                        [[DC * TB, 128], [CH, TB // CH], [1, CH - 1]]),
                bass.AP(sg, 1, [[TB, 128], [CH, TB // CH], [1, CH - 1]]),
                -1.0),
                waits=[(ACT, c_ln), (ACT, snap[ACT]), (GP, snap[GP])])
            c_d2 = P.op("vector", lambda e, dc=dc: e.memset(
                bass.AP(dtm, dc * TB, [[DC * TB, 128], [CH, TB // CH], [1, 1]]),
                BIGPOS))
            c_dx = P.op("vector", lambda e, dc=dc, tsl=tsl:
                        e.scalar_tensor_tensor(
                            xnm_t[:, :], xn_fm[:, dc, tsl], nw_c[:, dc:dc + 1],
                            rr_bc[:, tsl], ALU.mult, ALU.mult),
                        waits=[(ACT, c_rrbc)])
            c_d3 = P.op("vector", lambda e, sg=sg, dc=dc:
                        e.scalar_tensor_tensor(
                            u_bf[:, dc, :], sg[:, :], -1.0, xnm_t[:, :],
                            ALU.mult, ALU.mult))
            g_sg.set(dc, (DVE, c_d3))
        c_p2d = P.cnt[DVE]

        c_y0 = P.op("vector", lambda e: e.memset(y_acc[:, :, :], 0.0),
                    waits=[(GP, P.cnt[GP])])

        for n in range(NST):
            P.dma("sync", lambda e, n=n, t0=t0: e.dma_start(
                stage[0:1, 0, 0:TB], bc_sb[n:n + 1, t0:t0 + TB]),
                waits=g_stage.waits(n) + [(ACT, p1_done[ACT])])
            w_st = P.dma("sync", lambda e, n=n, t0=t0: e.dma_start(
                stage[0:1, 0, TB:2 * TB], bc_sb[n + 64:n + 65, t0:t0 + TB]))
            for half in range(2):
                P.op("tensor", lambda e, half=half: e.matmul(
                    bcb(0, half), ones_b[0:1, :],
                    stage[0:1, 0, TB + half * 512:TB + (half + 1) * 512],
                    start=True, stop=True),
                    waits=([(SP, w_st)] + g_bcB.waits(0)) if half == 0 else [],
                    inc=(1 if half == 1 else 0))
            c_mB = P.cnt[PE]
            for half in range(2):
                P.op("tensor", lambda e, half=half: e.matmul(
                    bcb(1, half), ones_b[0:1, :],
                    stage[0:1, 0, half * 512:(half + 1) * 512],
                    start=True, stop=True),
                    waits=g_bcC.waits(0) if half == 0 else [],
                    inc=(1 if half == 1 else 0))
            c_mC = P.cnt[PE]
            g_stage.set(n, (PE, c_mC))
            bB = bcB_r[0]
            bC = bcC_r[0]
            c_eB = P.op("scalar", lambda e, bB=bB: e.copy(
                bB[:, :], ps[:, 2 * 512:4 * 512]),
                waits=[(PE, c_mB)] + g_sbB.waits(n))
            g_bcB.set(0, (ACT, c_eB))
            c_eC = P.op("scalar", lambda e, bC=bC: e.copy(
                bC[:, :], ps[:, 4 * 512:6 * 512]),
                waits=[(PE, c_mC)] + g_sbC.waits(n))
            g_bcC.set(0, (ACT, c_eC))

            sB_cons = []
            sC_cons = []
            for dc in range(DC):
                da = da_r[uidx % 2]
                sb_ = s_r[uidx % 2]
                hb = hh_r[uidx % 2]
                zb = z_r[uidx % 2]
                c_exp = P.op("scalar", lambda e, da=da, dc=dc, n=n:
                             e.activation(da[:, :], dtm[:, dc, :], AF.Exp,
                                          scale=a_fm[:, dc, n:n + 1]),
                             waits=[(DVE, c_p2d)] + g_da.waits(uidx))
                c_s = P.op("vector", lambda e, sb_=sb_, dc=dc, bB=bB:
                           e.tensor_mul(sb_[:, :], u_bf[:, dc, :], bB[:, :]),
                           waits=[(ACT, c_eB), (DVE, c_p2d)] + g_s.waits(uidx))
                sB_cons.append((DVE, c_s))
                c_scan = P.op("vector", lambda e, hb=hb, da=da, sb_=sb_:
                              e.tensor_tensor_scan(hb[:, :], da[:, :],
                                                   sb_[:, :], 0.0,
                                                   ALU.mult, ALU.add),
                              waits=[(ACT, c_exp)] + g_hm.waits(uidx))
                g_da.set(uidx, (DVE, c_scan))
                g_s.set(uidx, (DVE, c_scan))
                c_z = P.op("vector", lambda e, zb=zb, hb=hb, bC=bC:
                           e.tensor_mul(zb[:, :], hb[:, :], bC[:, :]),
                           waits=[(ACT, c_eC)] + g_z.waits(uidx))
                zdep = (DVE, c_z)
                sC_cons.append(zdep)
                g_hm.set(uidx, zdep)
                c_y = P.op("vector", lambda e, dc=dc, zb=zb: e.tensor_add(
                    y_acc[:, dc, :], y_acc[:, dc, :], zb[:, :]),
                    waits=[(DVE, c_y0)])
                g_z.set(uidx, (DVE, c_y))
                uidx += 1
            g_sbB.set(n, *sB_cons)
            g_sbC.set(n, *sC_cons)

        if debug:
            P.dma("sync", lambda e, tb=tb: e.dma_start(
                h_ddtm.ap()[tb], dtm[:, :, :]), waits=[(DVE, c_p2d)])
            P.dma("sync", lambda e, tb=tb: e.dma_start(
                h_du.ap()[tb], u_bf[:, :, :]))
            P.dma("sync", lambda e, tb=tb: e.dma_start(
                h_dy.ap()[tb], y_acc[:, :, :]), waits=[(GP, P.cnt[GP])])

        c_gpd = P.cnt[GP]
        for dc in range(DC):
            P.op("vector", lambda e, dc=dc, tsl=tsl: e.scalar_tensor_tensor(
                ybf[:, dc, :], xn_fm[:, dc, tsl], dv_c[:, dc:dc + 1],
                y_acc[:, dc, :], ALU.mult, ALU.add),
                waits=[(GP, c_gpd), (PE, snap[PE])])
        c_ybf = P.cnt[DVE]
        if debug:
            P.dma("sync", lambda e, tb=tb: e.dma_start(
                h_dybf.ap()[tb], ybf[:, :, :]), waits=[(DVE, c_ybf)])
        for q in range(2):
            qsl = slice(t0 + q * 512, t0 + (q + 1) * 512)
            for dco in range(DC):
                it = q * DC + dco
                par = it % 2
                w_xf = P.dma("sync", lambda e, dco=dco, qsl=qsl, par=par:
                             e.dma_start(xfr_r[par][:, :],
                                         h_xfm.ap()[dco, :, qsl]),
                             waits=[(SP, p1_done[SP])] + g_xfr.waits(it))
                for dci in range(DC):
                    P.op("tensor", lambda e, dci=dci, dco=dco, q=q, par=par:
                         e.matmul(bank(6 + par, 512),
                                  opw_bf[:, dci, dco * 128:(dco + 1) * 128],
                                  ybf[:, dci, q * 512:(q + 1) * 512],
                                  start=(dci == 0), stop=(dci == DC - 1)),
                         waits=([(DVE, c_ybf), (ACT, opw_c)]
                                + g_op67.waits(it)) if dci == 0 else [],
                         inc=(1 if dci == DC - 1 else 0))
                c_op = P.cnt[PE]
                c_ev = P.op("vector", lambda e, par=par: e.tensor_add(
                    evr_r[par][:, :], bank(6 + par, 512), xfr_r[par][:, :]),
                    waits=[(PE, c_op), (SP, w_xf)] + g_evr.waits(it))
                g_op67.set(it, (DVE, c_ev))
                g_xfr.set(it, (DVE, c_ev))
                w_at = P.dma("sync", lambda e, dco=dco, qsl=qsl, par=par:
                             e.dma_start(h_attn.ap()[dco, :, qsl],
                                         evr_r[par][:, :]),
                             waits=[(DVE, c_ev)])
                g_evr.set(it, (SP, w_at))

    mixer_done = P.state()
    mx.close()

    fx = contextlib.ExitStack()

    def fb(name, shape, dtype):
        return fx.enter_context(nc.sbuf_tensor(name, shape, dtype))

    acc = fb("acc", [128, DC, T], F32)
    ffx = contextlib.ExitStack()

    def fb2(name, shape, dtype):
        return ffx.enter_context(nc.sbuf_tensor(name, shape, dtype))

    wgf_r = [fb2(f"wgf{k}", [128, 2, DC, 128], F32) for k in range(2)]
    wgb_r = [fb2(f"wgb{k}", [128, 2, DC, 128], BF16) for k in range(2)]
    wlf_r = [fb2("wlf0", [128, D], F32)]
    wl_e = fb2("wl_e", [128, KCN, D], BF16)
    g0s_r = [fb2(f"g0s{k}", [128, FT], BF16) for k in range(2)]
    g1s_r = [fb2(f"g1s{k}", [128, FT], BF16) for k in range(2)]
    t1_r = [fb2(f"t1{k}", [128, FT], BF16) for k in range(2)]
    hup_r = [fb2(f"hup{k}", [128, FT], BF16) for k in range(2)]
    hin_r = [fb2(f"hin{k}", [128, FT], BF16) for k in range(2)]
    cmb_bc = fb2("cmb_bc", [128, T], BF16)
    cstage = fb2("cstage", [1, TB], BF16)
    jnk2 = fb2("jnk2", [128, FT], F32) if True else None

    g_wgf = Guard(2)
    g_wgb = Guard(2)
    g_wlf = Guard(1)
    g_wle = Guard(1)
    g_gb = Guard(8)
    g_g0s = Guard(2)
    g_g1s = Guard(2)
    g_t1 = Guard(2)
    g_hup = Guard(2)
    g_hin = Guard(2)
    g_cst = Guard(1)
    g_cbc = Guard(1)
    g_dn = Guard(1)

    acc_w = []
    for dc in range(DC):
        w = P.dma("sync", lambda e, dc=dc: e.dma_start(
            acc[:, dc, :], h_attn.ap()[dc, :, :]),
            waits=[(DVE, mixer_done[DVE]), (SP, mixer_done[SP])])
        acc_w.append(w)

    hidx = 0
    hw_at = {}
    for e_i in range(E):
        c_cbc = 0
        for hf in range(2):
            w_cs = P.dma("sync", lambda e, e_i=e_i, hf=hf: e.dma_start(
                cstage[0:1, :], cmb_fm[e_i:e_i + 1, hf * TB:(hf + 1) * TB]),
                waits=[(ACT, p1_done[ACT]), (ACT, c_cbc)] + g_cst.waits(0))
            c_pe_cs = 0
            for q4 in range(2):
                c_bcm = P.op("tensor", lambda e, q4=q4: e.matmul(
                    bank(0, 512), ones_b[0:1, :],
                    cstage[0:1, q4 * 512:(q4 + 1) * 512],
                    start=True, stop=True),
                    waits=[(SP, w_cs), (ACT, c_cbc)] + g_gb.waits(0))
                c_cbc = P.op("scalar", lambda e, hf=hf, q4=q4: e.copy(
                    cmb_bc[:, hf * TB + q4 * 512:hf * TB + (q4 + 1) * 512],
                    bank(0, 512)), waits=[(PE, c_bcm)] + g_cbc.waits(0))
                g_gb.set(0, (ACT, c_cbc))
                c_pe_cs = c_bcm
            g_cst.set(0, (PE, c_pe_cs))

        if debug and e_i == 0:
            w_dbg = P.dma("sync", lambda e: e.dma_start(
                h_dcbc.ap(), cmb_bc[:, :]),
                waits=[(ACT, P.cnt["ACT"])])
            g_cbc.set(0, (SP, w_dbg))
        up_dve_last = 0
        for kc in range(KCN):
            par = kc % 2
            w1 = P.dma("sync", lambda e, e_i=e_i, kc=kc, par=par: e.dma_start(
                wgf_r[par][:, 0, :, :],
                h_gw.ap()[e_i, 0, :, :, kc * 128:(kc + 1) * 128]),
                waits=g_wgf.waits(kc))
            w2 = P.dma("sync", lambda e, e_i=e_i, kc=kc, par=par: e.dma_start(
                wgf_r[par][:, 1, :, :],
                h_gw.ap()[e_i, 1, :, :, kc * 128:(kc + 1) * 128]))
            w3 = P.dma("sync", lambda e, e_i=e_i, kc=kc, par=par: e.dma_start(
                wlf_r[0][:, :], h_lw.ap()[e_i, kc * 128:(kc + 1) * 128, :]),
                waits=g_wlf.waits(kc))
            cg = P.op("gpsimd", lambda e, par=par: e.tensor_copy(
                wgb_r[par][:, :, :, :], wgf_r[par][:, :, :, :]),
                waits=[(SP, w2)] + g_wgb.waits(kc))
            g_wgf.set(kc, (GP, cg))
            cl = P.op("gpsimd", lambda e, kc=kc: e.tensor_copy(
                wl_e[:, kc, :], wlf_r[0][:, :]),
                waits=[(SP, w3)] + (g_wle.waits(0) if kc == 0 else []))
            if debug and e_i == 0 and kc == 0:
                P.dma("sync", lambda e: e.dma_start(
                    h_dwg.ap(), wgb_r[0][:, :, :, :]), waits=[(GP, cg)])
            g_wlf.set(kc, (GP, cl))

            bank_done = {}
            for g in range(2):
                for dck in range(DC):
                    for ti4 in range(NFT):
                        b = g * 4 + ti4
                        c = P.op("tensor", lambda e, g=g, dck=dck, par=par,
                                 ti4=ti4, b=b: e.matmul(
                                     bank(b, 512), wgb_r[par][:, g, dck, :],
                                     xn_fm[:, dck, ti4 * FT:(ti4 + 1) * FT],
                                     start=(dck == 0), stop=(dck == DC - 1)),
                                 waits=([(GP, cg)] + g_gb.waits(b))
                                 if (dck == 0 and ti4 == 0) else
                                 (g_gb.waits(b) if dck == 0 else []),
                                 inc=(1 if dck == DC - 1 else 0))
                        if dck == DC - 1:
                            bank_done[b] = c
            for ti in range(NFT):
                fsl = slice(ti * FT, (ti + 1) * FT)
                g0s = g0s_r[ti % 2]
                g1s = g1s_r[ti % 2]
                t1b = t1_r[ti % 2]
                hup = hup_r[hidx % 2]
                c_e0 = P.op("scalar", lambda e, g0s=g0s, b=ti:
                            e.activation(g0s[:, :], bank(b, 512),
                                         AF.Gelu_apprx_tanh),
                            waits=[(PE, bank_done[ti])] + g_g0s.waits(ti))
                g_gb.set(ti, (ACT, c_e0))
                c_e1 = P.op("scalar", lambda e, g1s=g1s, b=4 + ti:
                            e.activation(g1s[:, :], bank(b, 512), AF.Copy),
                            waits=[(PE, bank_done[4 + ti])] + g_g1s.waits(ti))
                g_gb.set(4 + ti, (ACT, c_e1))
                if debug and e_i == 0 and kc == 0 and ti == 0:
                    P.dma("sync", lambda e: e.dma_start(
                        h_dg0.ap(), g0s[:, :]), waits=[(ACT, c_e0)])
                    P.dma("sync", lambda e: e.dma_start(
                        h_dg1.ap(), g1s[:, :]), waits=[(ACT, c_e1)])
                c_t1 = P.op("vector", lambda e, t1b=t1b, g1s=g1s, fsl=fsl:
                            e.tensor_mul(t1b[:, :], g1s[:, :], cmb_bc[:, fsl]),
                            waits=[(ACT, c_e1), (ACT, c_cbc)] + g_t1.waits(ti))
                g_g1s.set(ti, (DVE, c_t1))
                c_h = P.op("vector", lambda e, hup=hup, g0s=g0s, t1b=t1b:
                           e.tensor_mul(hup[:, :], g0s[:, :], t1b[:, :]),
                           waits=[(ACT, c_e0)] + g_hup.waits(hidx))
                g_g0s.set(ti, (DVE, c_h),
                          (("SP", P.cnt["SP"]) if (debug and e_i == 0
                           and kc == 0 and ti == 0) else None))
                g_t1.set(ti, (DVE, c_h))
                w_h = P.dma("scalar", lambda e, e_i=e_i, kc=kc, fsl=fsl,
                            hup=hup: e.dma_start(
                                h_h.ap()[e_i % 2, kc, :, fsl], hup[:, :]),
                            waits=[(DVE, c_h)])
                hw_at[(kc, ti)] = w_h
                g_hup.set(hidx, (AD, w_h))
                hidx += 1
                up_dve_last = c_h
            g_wgb.set(kc, (PE, P.cnt[PE]))
        c_up_act = P.cnt[ACT]
        c_up_ad = P.cnt[AD]
        g_cbc.set(0, (DVE, up_dve_last))

        for ti in range(NFT):
            fsl = slice(ti * FT, (ti + 1) * FT)
            for kc in range(KCN):
                hin = hin_r[(ti * KCN + kc) % 2]
                w_hi = P.dma("scalar", lambda e, e_i=e_i, kc=kc, fsl=fsl,
                             hin=hin: e.dma_start(
                                 hin[:, :], h_h.ap()[e_i % 2, kc, :, fsl]),
                             waits=[(AD, hw_at[(kc, ti)])]
                             + g_hin.waits(ti * KCN + kc))
                for dcm in range(DC):
                    P.op("tensor", lambda e, kc=kc, dcm=dcm, hin=hin:
                         e.matmul(
                             ps[:, dcm * 512:(dcm + 1) * 512],
                             wl_e[:, kc, dcm * 128:(dcm + 1) * 128],
                             hin[:, :],
                             start=(kc == 0), stop=(kc == KCN - 1),
                             skip_group_check=True),
                         waits=([(AD, w_hi), (GP, P.cnt[GP]),
                                 (ACT, c_up_act)] + g_dn.waits(0))
                         if (dcm == 0 and kc == 0) else
                         ([(AD, w_hi)] if dcm == 0 else []),
                         inc=(1 if dcm == DC - 1 else 0))
                g_hin.set(ti * KCN + kc, (PE, P.cnt[PE]))
            c_dn = P.cnt[PE]
            evs = []
            for dcm in range(DC):
                c_ev = P.op("vector", lambda e, dcm=dcm, ti=ti:
                            e.tensor_tensor(
                                acc[:, dcm, ti * FT:(ti + 1) * FT],
                                ps[:, dcm * 512:(dcm + 1) * 512],
                                acc[:, dcm, ti * FT:(ti + 1) * FT],
                                ALU.add),
                            waits=[(PE, c_dn), (SP, acc_w[dcm])])
                evs.append((DVE, c_ev))
            g_dn.set(0, *evs)
            for b in range(8):
                g_gb.set(b, *evs)
        g_wle.set(0, (PE, P.cnt[PE]))

    ffn_done = P.state()
    ffx.close()

    outt_r = [fx.enter_context(
        nc.sbuf_tensor(f"outt{k}", [128, DC, 128], F32)) for k in range(2)]
    g_ot = Guard(2)
    g_tb = Guard(2)
    for i in range(NTT):
        sl = slice(i * 128, (i + 1) * 128)
        ot = outt_r[i % 2]
        evl = 0
        for dc in range(DC):
            bidx = i * DC + dc
            c_mm = P.op("tensor", lambda e, dc=dc, sl=sl, b=bidx % 2:
                        e.matmul(bank(b, 128), acc[:, dc, sl], id_f[:, :],
                                 start=True, stop=True),
                        waits=[(DVE, ffn_done[DVE])] + g_tb.waits(bidx))
            c_ev = P.op("scalar", lambda e, ot=ot, dc=dc, b=bidx % 2: e.copy(
                ot[:, dc, :], bank(b, 128)),
                waits=[(PE, c_mm)] + (g_ot.waits(i) if dc == 0 else []))
            g_tb.set(bidx, (ACT, c_ev))
            evl = c_ev
        w_o = P.dma("sync", lambda e, ot=ot, sl=sl: e.dma_start(
            bass.AP(h_out, sl.start * D, [[D, 128], [128, DC], [1, 128]]),
            ot[:, :, :]), waits=[(ACT, evl)])
        g_ot.set(i, (SP, w_o))

    ctx = P.emit(nc)
    nc._keepalive = (gctx, fx, ctx)
    return nc


def kernel(L0_attn_x, L0_attn_norm_w, L0_attn_x_proj_w, L0_attn_dt_proj_w,
           L0_attn_dt_proj_b, L0_attn_A_log, L0_attn_D, L0_attn_out_proj_w,
           L0_ffn_router_scale, L0_ffn_router_w, L0_ffn_gating_w,
           L0_ffn_linear_w, L0_ffn_per_expert_scale):
    f32 = np.float32
    x = np.ascontiguousarray(np.asarray(L0_attn_x, f32))
    gw = np.asarray(L0_ffn_gating_w, f32).transpose(0, 1, 3, 2)
    gw5 = np.ascontiguousarray(
        gw.reshape(E, 2, DC, 128, HID).transpose(0, 1, 3, 2, 4))
    lw = np.ascontiguousarray(np.asarray(L0_ffn_linear_w, f32))
    xpw3 = np.ascontiguousarray(
        np.asarray(L0_attn_x_proj_w, f32).T.reshape(DC, 128, 192)
        .transpose(1, 0, 2))
    dtpwT = np.ascontiguousarray(np.asarray(L0_attn_dt_proj_w, f32).T)
    opwT = np.ascontiguousarray(np.asarray(L0_attn_out_proj_w, f32).T)
    alog3 = np.ascontiguousarray(
        np.asarray(L0_attn_A_log, f32).reshape(DC, 128, NST).transpose(1, 0, 2))

    def col(v):
        return np.asarray(v, f32).reshape(DC, 128).T

    sml = np.ascontiguousarray(np.stack(
        [col(L0_attn_D), col(L0_attn_dt_proj_b), col(L0_attn_norm_w),
         col(L0_ffn_router_scale), np.zeros((128, DC), f32)], axis=1))
    rw3 = np.ascontiguousarray(
        np.asarray(L0_ffn_router_w, f32).reshape(DC, 128, E).transpose(1, 0, 2))
    pes2 = np.ascontiguousarray(np.broadcast_to(
        np.asarray(L0_ffn_per_expert_scale, f32).reshape(1, E), (128, E)))

    nc = build_nc(debug=DEBUG)
    in_maps = []
    for c in range(NCORES):
        in_maps.append(dict(
            x=np.ascontiguousarray(x[c]), gw5=gw5, lw=lw, xpw3=xpw3,
            dtpwT=dtpwT, opwT=opwT, alog3=alog3, sml_d=sml, rw3=rw3, pes2=pes2))
    res = run_bass_kernel_spmd(nc, in_maps, core_ids=list(range(NCORES)))
    out = np.stack([res.results[c]["out"] for c in range(NCORES)], axis=0)
    return out
